# version 01
# baseline (speedup 1.0000x reference)
import os
os.environ.setdefault("TRN_ALLOW_ICI_ALL_TO_ALL", "1")
import numpy as np
import ml_dtypes
import concourse.bass as bass
import concourse.mybir as mybir
from concourse.bass_utils import run_bass_kernel_spmd

F32 = mybir.dt.float32
BF16 = mybir.dt.bfloat16
I32 = mybir.dt.int32
AF = mybir.ActivationFunctionType
ALU = mybir.AluOpType
AX = mybir.AxisListType
BF = ml_dtypes.bfloat16

B_, L, D = 2, 8192, 2048
E, N_ST, DT, DC = 4096, 16, 128, 4
I_FFN = 8192
NH, HN = 16, 128
EPS = 1e-5
NCORE = 8
EC = E // 4
NEB = EC // 128
TCH = 1024
NCHUNK = L // TCH
TQ = L // 4
ATT_C = 20.0
RGROUPS4 = [[0, 1, 2, 3], [4, 5, 6, 7]]
RGROUPS8 = [list(range(8))]

ENGS = ("sync", "vector", "scalar", "tensor", "gpsimd")


class CSem:
    def __init__(self, name):
        self.h = None
        self.name = name
        self.n = 0


class Prog:

    def __init__(self, nc, serial=False):
        self.nc = nc
        self.serial = serial
        self.ops = {e: [] for e in ENGS}
        self._sems = []
        self.chains = []
        self.chain_totals = []
        self._chain_tot = 0
        self._chain_idx = -1
        if serial:
            self._new_chain()

    def _new_chain(self):
        if self.chains:
            self.chain_totals.append(self._chain_tot)
        s = self.sem(f"chain{len(self.chains)}")
        self.chains.append(s)
        self._chain_idx += 1
        self._chain_tot = 0

    def sem(self, name):
        s = CSem(name)
        self._sems.append(s)
        return s

    def step(self, eng, fn, waits=(), incs=(), dma=False, collective=False):
        wait_list = [(s, v) for (s, v) in waits]
        my_incs = list(incs)
        if self.serial:
            if self._chain_tot > 24000:
                prev = self.chains[self._chain_idx]
                prev_tot = self._chain_tot
                self._new_chain()
                wait_list.append((prev, prev_tot))
            wait_list.append((self.chains[self._chain_idx], self._chain_tot))
            amt = 1 if collective else (16 if dma else 1)
            my_incs.insert(0, (self.chains[self._chain_idx], amt))
            self._chain_tot += amt

        def closure(e, fn=fn, wl=tuple(wait_list), mi=tuple(my_incs), coll=collective):
            for (s, v) in wl:
                e.wait_ge(s.h, v)
            inst = fn(e)
            if inst is None:
                for (s, a) in mi:
                    e.sem_inc(s.h, a)
                return None
            if mi:
                if coll:
                    inst.then_inc(mi[0][0].h)
                else:
                    inst.then_inc(mi[0][0].h, mi[0][1])
                for (s, a) in mi[1:]:
                    e.sem_inc(s.h, a)
            return inst

        self.ops[eng].append(closure)
        for (s, a) in incs:
            s.n += a
        if collective and incs:
            pass

    def emit(self):
        import contextlib
        nc = self.nc
        with contextlib.ExitStack() as stack:
            for s in self._sems:
                s.h = stack.enter_context(nc.semaphore(s.name))
            block = stack.enter_context(nc.Block())

            def runner(name):
                def _run(e):
                    for cl in self.ops[name]:
                        cl(e)
                return _run

            block.sync(runner("sync"))
            block.vector(runner("vector"))
            block.scalar(runner("scalar"))
            block.tensor(runner("tensor"))
            block.gpsimd(runner("gpsimd"))


class Carve:
    def __init__(self, t, cols, name=""):
        self.t, self.cols, self.pos, self.name = t, cols, 0, name

    def take_bf16(self, n):
        n_al = (n + 15) // 16 * 16
        assert self.pos + n_al <= self.cols, f"arena {self.name}: {self.pos}+{n_al}>{self.cols}"
        ap = self.t[:, self.pos:self.pos + n]
        self.pos += n_al
        return ap

    def take_f32(self, n):
        ap = self.take_bf16(2 * n)
        return ap.bitcast(F32)

    def mark(self):
        return self.pos

    def reset(self, pos=0):
        self.pos = pos


def _pack_lhsT(A):
    Ktot, Mtot = A.shape
    mt = max(Mtot // 128, 1)
    mcols = Mtot // mt
    return np.ascontiguousarray(A.reshape(Ktot // 128, 128, mt, mcols).transpose(2, 1, 0, 3))


def stage_inputs(inp):
    f = np.float32
    g = lambda k: np.asarray(inp[k], f)
    x = g("L0_attn_a_x")
    Wskip, Win = g("L0_attn_a_W_skip"), g("L0_attn_a_W_in")
    convw, convb = g("L0_attn_a_conv_w"), g("L0_attn_a_conv_b")
    Wd1, Wd2, bd2 = g("L0_attn_a_Wd1"), g("L0_attn_a_Wd2"), g("L0_attn_a_bd2")
    WBm, WCm, Alog, WD = g("L0_attn_a_WB"), g("L0_attn_a_WC"), g("L0_attn_a_A_log"), g("L0_attn_a_W_D")
    Wout, normw = g("L0_attn_a_W_out"), g("L0_attn_a_norm_w")
    Wqkv, Wproj, bproj = g("L0_attn_b_Wqkv"), g("L0_attn_b_Wproj"), g("L0_attn_b_bproj")
    wg, wu, wdn = g("L0_ffn_w_gate"), g("L0_ffn_w_up"), g("L0_ffn_w_down")

    xt = [np.ascontiguousarray(x[b].T) for b in range(B_)]
    wout_l = _pack_lhsT(Wout.T)
    normw_t = np.ascontiguousarray(normw.reshape(16, 128).T)
    wg_l = np.ascontiguousarray(wg.reshape(16, 128, 16, 128, 4).transpose(2, 4, 1, 0, 3))
    wu_l = np.ascontiguousarray(wu.reshape(16, 128, 16, 128, 4).transpose(2, 4, 1, 0, 3))
    wdn_l = np.ascontiguousarray(wdn.reshape(16, 128, 4, 16, 128).transpose(0, 2, 1, 3, 4))
    qrows = np.concatenate([Wqkv[h * 384:h * 384 + 128] for h in range(NH)], 0)
    krows = np.concatenate([Wqkv[h * 384 + 128:h * 384 + 256] for h in range(NH)], 0)
    vrows = np.concatenate([Wqkv[h * 384 + 256:h * 384 + 384] for h in range(NH)], 0)
    _qk = lambda Wr: np.ascontiguousarray(Wr.reshape(NH, 128, 16, 128).transpose(0, 3, 2, 1))
    wq_l, wk_l, wv_r = _qk(qrows), _qk(krows), _qk(vrows)
    wproj_l0 = _pack_lhsT(Wproj.T)
    wproj_lb = [wproj_l0,
                np.ascontiguousarray(wproj_l0[:, :, list(range(8, 16)) + list(range(8)), :])]
    bproj_t = np.ascontiguousarray(bproj.reshape(16, 128).T)

    in_maps = []
    for c in range(NCORE):
        b, sh = c // 4, c % 4
        es = slice(sh * EC, (sh + 1) * EC)
        in_maps.append({
            "xt": xt[b],
            "xqt": np.ascontiguousarray(x[b, sh * TQ:(sh + 1) * TQ, :].T),
            "wskip_l": _pack_lhsT(Wskip[es].T),
            "win_l": _pack_lhsT(Win[es].T),
            "convw": np.ascontiguousarray(convw[es, 0, :].reshape(NEB, 128, DC)),
            "convb_t": np.ascontiguousarray(convb[es].reshape(NEB, 128).T),
            "bd2_t": np.ascontiguousarray(bd2[es].reshape(NEB, 128).T),
            "wd_t": np.ascontiguousarray(WD[es].reshape(NEB, 128).T),
            "wd1_l": np.ascontiguousarray(_pack_lhsT(Wd1.T[es]).reshape(128, NEB, 128)),
            "wd2_l": np.ascontiguousarray(_pack_lhsT(Wd2[es].T).reshape(NEB, 128, 128)),
            "wbc_l": np.ascontiguousarray(
                _pack_lhsT(np.concatenate([WBm, WCm], 0).T[es]).reshape(128, NEB, 32)),
            "alog": np.ascontiguousarray(Alog[es].reshape(NEB, 128, N_ST)),
            "wout_l": wout_l,
            "normw_t": normw_t,
            "wg_l": wg_l, "wu_l": wu_l, "wdn_l": wdn_l,
            "wq_l": wq_l, "wk_l": wk_l, "wv_r": wv_r,
            "wproj_l": wproj_lb[c // 4], "bproj_t": bproj_t,
            "params": np.array([4 * (c // 4), (1 - c // 4) * 8 * HN * TQ,
                                (c // 4) * 8 * HN * TQ, (c ^ 4) * 8 * HN * TQ,
                                0, 0, 0, 0], dtype=np.int32),
        })
    return in_maps


def build(serial=True, upto="all", taps=()):
    nc = bass.Bass("TRN2", target_bir_lowering=False)
    want = lambda t: t in taps

    din = lambda name, shape, dt=F32: nc.dram_tensor(name, shape, dt, kind="ExternalInput").ap()
    xt = din("xt", [D, L])
    xqt = din("xqt", [D, TQ])
    wskip_l = din("wskip_l", [NEB, 128, 16, 128])
    win_l = din("win_l", [NEB, 128, 16, 128])
    convw = din("convw", [NEB, 128, DC])
    convb_t = din("convb_t", [128, NEB])
    bd2_t = din("bd2_t", [128, NEB])
    wd_t = din("wd_t", [128, NEB])
    wd1_l = din("wd1_l", [128, NEB, 128])
    wd2_l = din("wd2_l", [NEB, 128, 128])
    wbc_l = din("wbc_l", [128, NEB, 32])
    alog = din("alog", [NEB, 128, N_ST])
    wout_l = din("wout_l", [16, 128, 32, 128])
    normw_t = din("normw_t", [128, 16])
    wg_l = din("wg_l", [16, 4, 128, 16, 128])
    wu_l = din("wu_l", [16, 4, 128, 16, 128])
    wdn_l = din("wdn_l", [16, 4, 128, 16, 128])
    wq_l = din("wq_l", [NH, 128, 16, 128])
    wk_l = din("wk_l", [NH, 128, 16, 128])
    wv_r = din("wv_r", [NH, 128, 16, 128])
    wproj_l = din("wproj_l", [16, 128, 16, 128])
    bproj_t = din("bproj_t", [128, 16])
    params = din("params", [8], I32)

    out_t = nc.dram_tensor("out_t", [D, TQ], F32, kind="ExternalOutput").ap()

    tap_ap = {}
    def mktap(name, shape, dt=F32):
        if want(name):
            tap_ap[name] = nc.dram_tensor("tap_" + name, shape, dt, kind="ExternalOutput").ap()
    mktap("xn", [NCHUNK, D, TCH], BF16)
    mktap("u", [NCHUNK, EC, TCH], BF16)
    mktap("sskip", [NCHUNK, EC, TCH], BF16)
    mktap("delta", [NCHUNK, EC, TCH], BF16)
    mktap("bc", [NCHUNK, 32, TCH])
    mktap("arin", [NCHUNK, 160, TCH])
    mktap("arout", [NCHUNK, 160, TCH])
    mktap("y", [NCHUNK, EC, TCH], BF16)
    mktap("yt", [NCHUNK, EC, TCH], BF16)
    mktap("x1", [D, TQ], BF16)
    mktap("h_ffn", [I_FFN, TQ], BF16)
    mktap("hs_ffn", [I_FFN, TQ], BF16)
    mktap("x2", [D, TQ], BF16)
    mktap("xn3", [D, TQ], BF16)
    mktap("x3", [D, TQ], BF16)

    dscr = lambda name, shape, dt: nc.dram_tensor(name, shape, dt, kind="Internal").ap()
    sskip_d = dscr("sskip_d", [NEB, 128, L], BF16)
    ar_in = dscr("ar_in", [NCHUNK, 160, TCH], F32)
    ar_out = dscr("ar_out", [NCHUNK, 160, TCH], F32)
    a2a_in = dscr("a2a_in", [8 * EC, TQ], BF16)
    a2a_out = dscr("a2a_out", [8 * EC, TQ], BF16)
    x1_d = dscr("x1_d", [D, TQ], BF16)
    x2_d = dscr("x2_d", [D, TQ], BF16)
    x3_d = dscr("x3_d", [D, TQ], BF16)
    kag_in = dscr("kag_in", [NH * HN, TQ], BF16)
    kag_out = dscr("kag_out", [8 * NH * HN, TQ], BF16)
    vag_in = dscr("vag_in", [NH * TQ, HN], BF16)
    vag_out = dscr("vag_out", [8 * NH * TQ, HN], BF16)
    q_d = dscr("q_d", [NH * HN, TQ], BF16)
    a2a_stage = dscr("a2a_stage", [4 * EC, TQ], BF16)
    ytq_d = dscr("ytq_d", [4 * EC, TQ], BF16)
    kq_d = dscr("kq_d", [4 * NH * HN, TQ], BF16)
    vq_d = dscr("vq_d", [4 * NH * TQ, HN], BF16)
    agc_in = dscr("agc_in", [8 * HN, TQ], BF16)
    agc_out = dscr("agc_out", [8 * 8 * HN, TQ], BF16)
    ctxp_d = dscr("ctxp_d", [16 * HN, TQ], BF16)

    AR_COLS = 96_256
    arena_t = nc.alloc_sbuf_tensor("arena", [128, AR_COLS], BF16)
    A = Carve(arena_t, AR_COLS, "A")

    consts = nc.alloc_sbuf_tensor("consts", [128, 256], BF16)
    onesb = consts[:, 0:128]
    nw_sb = nc.alloc_sbuf_tensor("nw_sb", [128, 16], F32)
    cvb_sb = nc.alloc_sbuf_tensor("cvb_sb", [128, NEB], F32)
    bd2_sb = nc.alloc_sbuf_tensor("bd2_sb", [128, NEB], F32)
    wdcol_sb = nc.alloc_sbuf_tensor("wdcol_sb", [128, NEB], F32)
    cw_sb = nc.alloc_sbuf_tensor("cw_sb", [128, NEB * DC], F32)
    a_sb = nc.alloc_sbuf_tensor("a_sb", [128, NEB * N_ST], F32)
    state_sb = nc.alloc_sbuf_tensor("state_sb", [128, NEB * N_ST], F32)
    halo_sb = nc.alloc_sbuf_tensor("halo_sb", [128, NEB * (DC - 1)], BF16)
    rows = nc.alloc_sbuf_tensor("rows", [1, 3 * TCH], F32)
    rows2 = nc.alloc_sbuf_tensor("rows2", [1, 2 * TCH], BF16)
    psel = nc.alloc_sbuf_tensor("psel", [1, 8], I32)
    bprj_sb = nc.alloc_sbuf_tensor("bprj_sb", [128, 16], F32)
    cst32 = nc.alloc_sbuf_tensor("cst32", [128, 2], F32)

    PS = nc.alloc_psum_tensor("PS", [128, 4096], F32)
    def ps(i, w=512, r0=0, nr=128):
        return PS[r0:r0 + nr, 512 * i:512 * i + w]

    P = Prog(nc, serial=serial)
    st = P.sem("st")
    ld = P.sem("ld")
    cc = P.sem("cc")

    mu_b = A.take_bf16(TCH)
    s_b = A.take_bf16(TCH)
    sq_c = A.take_bf16(TCH)
    evtmp = A.take_bf16(2 * TCH)
    COMMON = A.mark()

    def dma(eng, out, in_, **kw):
        assert int(np.prod(out.shape)) == int(np.prod(in_.shape)), \
            f"dma size mismatch: out{tuple(out.shape)} in{tuple(in_.shape)}"
        P.step(eng, lambda e, out=out, in_=in_, kw=kw: e.dma_start(out=out, in_=in_, **kw),
               incs=[(ld, 16)], dma=True)

    def store(out, in_, eng="sync"):
        P.step(eng, lambda e, out=out, in_=in_: e.dma_start(out=out, in_=in_),
               incs=[(st, 16)], dma=True)

    def vec(fn):
        P.step("vector", fn)

    def pool(fn):
        P.step("gpsimd", fn)

    def act(fn):
        P.step("scalar", fn)

    def mm(fn):
        P.step("tensor", fn)

    regs = {}
    def dyn_off(e, key, mul, base, pidx=0):
        if key not in regs:
            rb = e.alloc_register(key + "_b")
            e.reg_load(rb, psel[0:1, pidx:pidx + 1])
            e.reg_mul(rb, rb, mul)
            ra = e.alloc_register(key + "_a")
            regs[key] = (rb, ra)
        rb, ra = regs[key]
        e.reg_add(ra, rb, base)
        return ra

    dma("sync", nw_sb[:], normw_t)
    dma("sync", cvb_sb[:], convb_t)
    dma("sync", bd2_sb[:], bd2_t)
    dma("sync", wdcol_sb[:], wd_t)
    dma("sync", cw_sb[:].rearrange("p (b c) -> p b c", b=NEB), convw.rearrange("b p c -> p b c"))
    dma("sync", bprj_sb[:], bproj_t)
    dma("sync", psel[:], params[None, :])
    dma("sync", a_sb[:].rearrange("p (b n) -> p b n", b=NEB), alog.rearrange("b p n -> p b n"))
    vec(lambda e: e.memset(consts[:, :], 0.0))
    vec(lambda e: e.memset(onesb, 1.0))
    act(lambda e: e.activation(a_sb[:], a_sb[:], AF.Exp))
    vec(lambda e: e.tensor_scalar(a_sb[:], a_sb[:], -1.0, None, ALU.mult))
    vec(lambda e: e.memset(cst32[:, 0:1], -ATT_C))
    vec(lambda e: e.memset(state_sb[:], 0.0))
    vec(lambda e: e.memset(halo_sb[:], 0.0))

    xb_c = A.take_bf16(16 * TCH)
    u_c = [A.take_bf16(NEB * TCH) for _ in range(2)]
    dl_c = A.take_bf16(NEB * TCH)
    upre_c = A.take_bf16(NEB * TCH)
    d1ar_c = A.take_bf16(TCH)
    bb_c = A.take_bf16(2 * TCH)
    cbx_c = A.take_bf16(2 * TCH)
    dA_c = A.take_bf16(2 * TCH)
    d1t_c = A.take_bf16(2 * TCH)
    h_c = A.take_bf16(2 * TCH)
    w_c = A.take_bf16(TCH)
    yac_c = A.take_bf16(TCH)
    wt = [A.take_bf16(16 * 128) for _ in range(4)]
    sk_c = A.take_bf16(TCH)
    mamba_mark = A.mark()

    def ln_norm(src_dram, src_is_bf16, tok0, T, xb_dst, gamma_col, extra_rms):
        ev32 = evtmp.bitcast(F32)
        for kc in range(16):
            xv = xb_dst[:, kc * T:(kc + 1) * T]
            if src_is_bf16:
                dma("sync", xv, src_dram[kc * 128:(kc + 1) * 128, tok0:tok0 + T])
            else:
                dma("sync", ev32[:, 0:T], src_dram[kc * 128:(kc + 1) * 128, tok0:tok0 + T])
                vec(lambda e, xv=xv: e.tensor_copy(xv, ev32[:, 0:T]))
            act(lambda e, xv=xv: e.activation(sq_c[:, 0:T], xv, AF.Square))
            for h2 in range(T // 512):
                mm(lambda e, kc=kc, h2=h2, xv=xv: e.matmul(
                    ps(4 + h2, nr=1), onesb[:, 0:1], xv[:, h2 * 512:h2 * 512 + 512],
                    start=(kc == 0), stop=(kc == 15)))
                mm(lambda e, kc=kc, h2=h2: e.matmul(
                    ps(6 + h2, nr=1), onesb[:, 0:1], sq_c[:, h2 * 512:h2 * 512 + 512],
                    start=(kc == 0), stop=(kc == 15)))
        r_mu, r_v, r_t = (rows[0:1, i * TCH:i * TCH + T] for i in range(3))
        r_s = r_v
        vec(lambda e: e.tensor_scalar(r_mu, PS[0:1, 2048:2048 + T], 1.0 / D, None, ALU.mult))
        vec(lambda e: e.tensor_scalar(r_v, PS[0:1, 3072:3072 + T], 1.0 / D, None, ALU.mult))
        vec(lambda e: e.tensor_tensor(r_t, r_mu, r_mu, ALU.mult))
        vec(lambda e: e.tensor_tensor(r_v, r_v, r_t, ALU.subtract))
        vec(lambda e: e.tensor_scalar(r_t, r_v, EPS, None, ALU.add))
        vec(lambda e: e.reciprocal(r_t, r_t))
        if extra_rms:
            vec(lambda e: e.tensor_tensor(r_v, r_v, r_t, ALU.mult))
            vec(lambda e: e.tensor_scalar(r_v, r_v, EPS, None, ALU.add))
            vec(lambda e: e.reciprocal(r_v, r_v))
            vec(lambda e: e.tensor_tensor(r_v, r_v, r_t, ALU.mult))
            act(lambda e: e.activation(r_s, r_v, AF.Sqrt))
        else:
            act(lambda e: e.activation(r_s, r_t, AF.Sqrt))
        vec(lambda e: e.tensor_copy(rows2[0:1, 0:T], r_mu))
        vec(lambda e: e.tensor_copy(rows2[0:1, TCH:TCH + T], r_s))
        for h2 in range(T // 512):
            mm(lambda e, h2=h2: e.matmul(ps(4 + h2), onesb[0:1, 0:128],
                                         rows2[0:1, h2 * 512:h2 * 512 + 512],
                                         start=True, stop=True))
            mm(lambda e, h2=h2: e.matmul(ps(6 + h2), onesb[0:1, 0:128],
                                         rows2[0:1, TCH + h2 * 512:TCH + h2 * 512 + 512],
                                         start=True, stop=True))
        vec(lambda e: e.tensor_copy(mu_b[:, 0:T], PS[:, 2048:2048 + T]))
        vec(lambda e: e.tensor_copy(s_b[:, 0:T], PS[:, 3072:3072 + T]))
        for kc in range(16):
            xv = xb_dst[:, kc * T:(kc + 1) * T]
            vec(lambda e, xv=xv: e.tensor_tensor(xv, xv, mu_b[:, 0:T], ALU.subtract))
            if gamma_col is not None:
                vec(lambda e, xv=xv, kc=kc: e.scalar_tensor_tensor(
                    xv, xv, gamma_col[:, kc:kc + 1], s_b[:, 0:T], ALU.mult, ALU.mult))
            else:
                vec(lambda e, xv=xv: e.tensor_tensor(xv, xv, s_b[:, 0:T], ALU.mult))

    def mamba():
        for c in range(NCHUNK):
            uc = u_c[c % 2]
            t0 = c * TCH
            ln_norm(xt, False, t0, TCH, xb_c, nw_sb, extra_rms=True)
            if want("xn"):
                store(tap_ap["xn"][c].rearrange("(k p) t -> p k t", p=128), xb_c[:].rearrange("p (k t) -> p k t", k=16))
            for eb in range(NEB):
                dma("gpsimd", wt[0][:], wskip_l[eb].rearrange("p k c -> p (k c)"))
                dma("gpsimd", wt[1][:], win_l[eb].rearrange("p k c -> p (k c)"))
                for tt in range(2):
                    for kc in range(16):
                        mm(lambda e, tt=tt, kc=kc: e.matmul(
                            ps(0 + tt), wt[0][:, kc * 128:kc * 128 + 128],
                            xb_c[:, kc * TCH + tt * 512:kc * TCH + tt * 512 + 512],
                            start=(kc == 0), stop=(kc == 15)))
                act(lambda e: e.activation(sk_c[:, 0:512], ps(0), AF.Silu))
                act(lambda e: e.activation(sk_c[:, 512:1024], ps(1), AF.Silu))
                dma("sync", sskip_d[eb, :, t0:t0 + TCH], sk_c[:, 0:TCH])
                if want("sskip"):
                    store(tap_ap["sskip"][c, eb * 128:(eb + 1) * 128, :], sk_c[:, 0:TCH])
                for tt in range(2):
                    for kc in range(16):
                        mm(lambda e, tt=tt, kc=kc: e.matmul(
                            ps(2 + tt), wt[1][:, kc * 128:kc * 128 + 128],
                            xb_c[:, kc * TCH + tt * 512:kc * TCH + tt * 512 + 512],
                            start=(kc == 0), stop=(kc == 15)))
                vec(lambda e, eb=eb: e.tensor_copy(upre_c[:, eb * TCH:eb * TCH + 512], ps(2)))
                vec(lambda e, eb=eb: e.tensor_copy(upre_c[:, eb * TCH + 512:eb * TCH + 1024], ps(3)))
            for eb in range(NEB):
                up = upre_c[:, eb * TCH:(eb + 1) * TCH]
                uv = uc[:, eb * TCH:(eb + 1) * TCH]
                hl = halo_sb[:, eb * 3:eb * 3 + 3]
                vec(lambda e, up=up, uv=uv, eb=eb: e.tensor_scalar(
                    uv, up, cw_sb[:, eb * DC + 3:eb * DC + 4], None, ALU.mult))
                for j in range(3):
                    s = 3 - j
                    vec(lambda e, up=up, uv=uv, eb=eb, j=j, s=s: e.scalar_tensor_tensor(
                        uv[:, s:TCH], up[:, 0:TCH - s], cw_sb[:, eb * DC + j:eb * DC + j + 1],
                        uv[:, s:TCH], ALU.mult, ALU.add))
                    for bcol in range(s):
                        hidx = 3 + bcol - s
                        vec(lambda e, uv=uv, hl=hl, eb=eb, j=j, bcol=bcol, hidx=hidx:
                            e.scalar_tensor_tensor(
                                uv[:, bcol:bcol + 1], hl[:, hidx:hidx + 1],
                                cw_sb[:, eb * DC + j:eb * DC + j + 1],
                                uv[:, bcol:bcol + 1], ALU.mult, ALU.add))
                vec(lambda e, up=up, hl=hl: e.tensor_copy(hl, up[:, TCH - 3:TCH]))
                act(lambda e, uv=uv, eb=eb: e.activation(uv, uv, AF.Silu,
                                                         bias=cvb_sb[:, eb:eb + 1]))
            if want("u"):
                for eb in range(NEB):
                    store(tap_ap["u"][c, eb * 128:(eb + 1) * 128, :],
                          uc[:, eb * TCH:(eb + 1) * TCH])
            dma("gpsimd", wt[2][:, 0:NEB * 128], wd1_l.rearrange("p b c -> p (b c)"))
            dma("gpsimd", wt[3][:, 0:NEB * 32], wbc_l.rearrange("p b c -> p (b c)"))
            for tt in range(2):
                for eb in range(NEB):
                    mm(lambda e, tt=tt, eb=eb, uc=uc: e.matmul(
                        ps(0 + tt), wt[2][:, eb * 128:eb * 128 + 128],
                        uc[:, eb * TCH + tt * 512:eb * TCH + tt * 512 + 512],
                        start=(eb == 0), stop=(eb == NEB - 1)))
                for eb in range(NEB):
                    mm(lambda e, tt=tt, eb=eb, uc=uc: e.matmul(
                        ps(2 + tt, nr=32), wt[3][:, eb * 32:eb * 32 + 32],
                        uc[:, eb * TCH + tt * 512:eb * TCH + tt * 512 + 512],
                        start=(eb == 0), stop=(eb == NEB - 1)))
            ev32 = evtmp.bitcast(F32)
            vec(lambda e: e.tensor_copy(ev32[:, 0:512], ps(0)))
            vec(lambda e: e.tensor_copy(ev32[:, 512:1024], ps(1)))
            dma("sync", ar_in[c, 0:128, :], ev32[:, 0:TCH])
            vec(lambda e: e.tensor_copy(ev32[0:32, 0:512], ps(2, nr=32)))
            vec(lambda e: e.tensor_copy(ev32[0:32, 512:1024], ps(3, nr=32)))
            dma("sync", ar_in[c, 128:160, :], ev32[0:32, 0:TCH])
            if want("arin"):
                store(tap_ap["arin"][c], ar_in[c])
            P.step("gpsimd", lambda e, c=c: e.collective_compute(
                "AllReduce", ALU.add, replica_groups=RGROUPS4,
                ins=[ar_in[c].opt()], outs=[ar_out[c].opt()]),
                incs=[(cc, 1)], collective=True)
            if want("bc"):
                store(tap_ap["bc"][c], ar_out[c, 128:160, :])
            if want("arout"):
                store(tap_ap["arout"][c], ar_out[c])
            dma("gpsimd", d1ar_c[:], ar_out[c, 0:128, :])
            dma("gpsimd", wt[2][:, 0:NEB * 128].rearrange("p (b c) -> p b c", b=NEB), wd2_l.rearrange("b p c -> p b c"))
            for eb in range(NEB):
                for tt in range(2):
                    mm(lambda e, eb=eb, tt=tt: e.matmul(
                        ps(4 + tt), wt[2][:, eb * 128:eb * 128 + 128],
                        d1ar_c[:, tt * 512:tt * 512 + 512], start=True, stop=True))
                dv = dl_c[:, eb * TCH:(eb + 1) * TCH]
                act(lambda e, dv=dv, eb=eb: e.activation(
                    dv[:, 0:512], ps(4), AF.Exp, bias=bd2_sb[:, eb:eb + 1]))
                act(lambda e, dv=dv, eb=eb: e.activation(
                    dv[:, 512:1024], ps(5), AF.Exp, bias=bd2_sb[:, eb:eb + 1]))
                act(lambda e, dv=dv: e.activation(dv, dv, AF.Ln, bias=1.0))
            if want("delta"):
                for eb in range(NEB):
                    store(tap_ap["delta"][c, eb * 128:(eb + 1) * 128, :],
                          dl_c[:, eb * TCH:(eb + 1) * TCH])
            for eb in range(NEB):
                dv = dl_c[:, eb * TCH:(eb + 1) * TCH]
                uv = uc[:, eb * TCH:(eb + 1) * TCH]
                vec(lambda e, dv=dv, uv=uv: e.tensor_tensor(w_c[:, 0:TCH], dv, uv, ALU.mult))
                for n in range(N_ST):
                    r = n % 2
                    bb = bb_c[:, r * TCH:(r + 1) * TCH]
                    cbv = cbx_c[:, r * TCH:(r + 1) * TCH]
                    dav = dA_c[:, r * TCH:(r + 1) * TCH]
                    d1v = d1t_c[:, r * TCH:(r + 1) * TCH]
                    hv = h_c[:, r * TCH:(r + 1) * TCH]
                    dma("gpsimd", bb, ar_out[c, 128 + n:129 + n, :].partition_broadcast(128))
                    dma("gpsimd", cbv, ar_out[c, 144 + n:145 + n, :].partition_broadcast(128))
                    act(lambda e, dav=dav, dv=dv, eb=eb, n=n: e.activation(
                        dav, dv, AF.Exp, scale=a_sb[:, eb * N_ST + n:eb * N_ST + n + 1]))
                    pool(lambda e, d1v=d1v, bb=bb: e.tensor_tensor(
                        d1v, w_c[:, 0:TCH], bb, ALU.mult))
                    sidx = eb * N_ST + n
                    vec(lambda e, hv=hv, dav=dav, d1v=d1v, sidx=sidx: e.tensor_tensor_scan(
                        hv, dav, d1v, state_sb[:, sidx:sidx + 1], ALU.mult, ALU.add))
                    vec(lambda e, hv=hv, sidx=sidx: e.tensor_copy(
                        state_sb[:, sidx:sidx + 1], hv[:, TCH - 1:TCH]))
                    if n == 0:
                        vec(lambda e, hv=hv, cbv=cbv: e.tensor_tensor(
                            yac_c[:, 0:TCH], hv, cbv, ALU.mult))
                    else:
                        vec(lambda e, hv=hv, cbv=cbv, r=r: e.tensor_tensor(
                            evtmp[:, r * TCH:(r + 1) * TCH], hv, cbv, ALU.mult))
                        vec(lambda e, r=r: e.tensor_tensor(
                            yac_c[:, 0:TCH], yac_c[:, 0:TCH],
                            evtmp[:, r * TCH:(r + 1) * TCH], ALU.add))
                if want("y"):
                    store(tap_ap["y"][c, eb * 128:(eb + 1) * 128, :], yac_c[:, 0:TCH])
                dma("sync", sk_c[:, 0:TCH], sskip_d[eb, :, t0:t0 + TCH])
                vec(lambda e, uv=uv, eb=eb: e.scalar_tensor_tensor(
                    yac_c[:, 0:TCH], uv, wdcol_sb[:, eb:eb + 1], yac_c[:, 0:TCH],
                    ALU.mult, ALU.add))
                vec(lambda e: e.tensor_tensor(yac_c[:, 0:TCH], yac_c[:, 0:TCH],
                                              sk_c[:, 0:TCH], ALU.mult))
                if want("yt"):
                    store(tap_ap["yt"][c, eb * 128:(eb + 1) * 128, :], yac_c[:, 0:TCH])
                dest_block = c // 2
                col0 = (c % 2) * TCH
                row0 = dest_block * EC + eb * 128
                dma("sync", a2a_stage[row0:row0 + 128, col0:col0 + TCH], yac_c[:, 0:TCH])

    def mamba_out():
        def nrm_in(e):
            r = dyn_off(e, "sy_a2a", EC * TQ, 0)
            ap = bass.AP(a2a_in.tensor, r, [[1, 4 * EC * TQ]])
            return e.dma_start(out=ap, in_=a2a_stage.rearrange("r t -> (r t)"))
        P.step("sync", nrm_in, incs=[(ld, 16)], dma=True)
        def a2a_coll(e):
            from concourse.aot_env import allow_ici_all_to_all
            with allow_ici_all_to_all():
                return e.collective_compute(
                    "AllToAll", ALU.bypass, replica_groups=RGROUPS8,
                    ins=[a2a_in.opt()], outs=[a2a_out.opt()])
        P.step("gpsimd", a2a_coll, incs=[(cc, 1)], collective=True)
        def nrm_out(e):
            r = dyn_off(e, "sy_a2a2", EC * TQ, 0)
            ap = bass.AP(a2a_out.tensor, r, [[1, 4 * EC * TQ]])
            return e.dma_start(out=ytq_d.rearrange("r t -> (r t)"), in_=ap)
        P.step("sync", nrm_out, incs=[(ld, 16)], dma=True)
        A.reset(COMMON)
        yt_t = [A.take_bf16(TQ) for _ in range(2)]
        wo_t = [A.take_bf16(32 * 128) for _ in range(2)]
        xq_t = A.take_f32(512)
        x1t = A.take_bf16(TQ)
        for db in range(16):
            dma("gpsimd", wo_t[db % 2][:], wout_l[db].rearrange("p k c -> p (k c)"))
            for kc in range(32):
                dma("sync", yt_t[kc % 2][:], ytq_d[kc * 128:(kc + 1) * 128, :])
                for tt in range(4):
                    mm(lambda e, db=db, kc=kc, tt=tt: e.matmul(
                        ps(tt), wo_t[db % 2][:, kc * 128:kc * 128 + 128],
                        yt_t[kc % 2][:, tt * 512:tt * 512 + 512],
                        start=(kc == 0), stop=(kc == 31)))
            for tt in range(4):
                dma("sync", xq_t[:, 0:512], xqt[db * 128:(db + 1) * 128, tt * 512:tt * 512 + 512])
                vec(lambda e, tt=tt: e.tensor_tensor(
                    x1t[:, tt * 512:tt * 512 + 512], ps(tt), xq_t[:, 0:512], ALU.add))
            dma("sync", x1_d[db * 128:(db + 1) * 128, :], x1t[:])
            if want("x1"):
                store(tap_ap["x1"][db * 128:(db + 1) * 128, :], x1t[:])

    def ffn(src_d, dst_d, final_out):
        A.reset(COMMON)
        h_t = A.take_bf16(8 * 4 * TCH)
        acc32 = A.take_f32(16 * TCH)
        xn2 = A.take_bf16(16 * TCH)
        wgt = A.take_bf16(16 * 128)
        wut = A.take_bf16(16 * 128)
        wdt = [A.take_bf16(16 * 128)]
        thr_t = A.take_bf16(TCH)
        m1_t = A.take_bf16(2 * TCH)
        res_t = thr_t[:, 0:512]
        o32 = m1_t.bitcast(F32)[:, 0:512]

        def hview(ib8, m):
            return h_t[:, (ib8 * 4 + m) * TCH:(ib8 * 4 + m + 1) * TCH]

        for half in range(2):
            t0 = half * TCH
            ln_norm(src_d, True, t0, TCH, xn2, None, extra_rms=False)
            for ih in range(2):
                for ib8 in range(8):
                    ib = ih * 8 + ib8
                    for m in range(4):
                        dma("gpsimd", wgt[:], wg_l[ib, m].rearrange("p k c -> p (k c)"))
                        dma("gpsimd", wut[:], wu_l[ib, m].rearrange("p k c -> p (k c)"))
                        for tt in range(2):
                            for kc in range(16):
                                mm(lambda e, tt=tt, kc=kc: e.matmul(
                                    ps(0 + tt), wgt[:, kc * 128:kc * 128 + 128],
                                    xn2[:, kc * TCH + tt * 512:kc * TCH + tt * 512 + 512],
                                    start=(kc == 0), stop=(kc == 15)))
                            for kc in range(16):
                                mm(lambda e, tt=tt, kc=kc: e.matmul(
                                    ps(2 + tt), wut[:, kc * 128:kc * 128 + 128],
                                    xn2[:, kc * TCH + tt * 512:kc * TCH + tt * 512 + 512],
                                    start=(kc == 0), stop=(kc == 15)))
                        hv = hview(ib8, m)
                        act(lambda e, hv=hv: e.activation(hv[:, 0:512], ps(0), AF.Silu))
                        act(lambda e, hv=hv: e.activation(hv[:, 512:1024], ps(1), AF.Silu))
                        vec(lambda e, hv=hv: e.tensor_tensor(hv[:, 0:512], hv[:, 0:512],
                                                             ps(2), ALU.mult))
                        vec(lambda e, hv=hv: e.tensor_tensor(hv[:, 512:1024], hv[:, 512:1024],
                                                             ps(3), ALU.mult))
                    h0, h1 = hview(ib8, 0), hview(ib8, 1)
                    h2v, h3 = hview(ib8, 2), hview(ib8, 3)
                    ab = [m1_t[:, 0:TCH], m1_t[:, TCH:2 * TCH]]
                    aa = [sq_c[:, 0:TCH], evtmp[:, 0:TCH], evtmp[:, TCH:2 * TCH], thr_t[:, 0:TCH]]
                    for idx4, hv in enumerate((h0, h1, h2v, h3)):
                        act(lambda e, hv=hv, idx4=idx4: e.activation(aa[idx4], hv, AF.Abs))
                    vec(lambda e: e.tensor_tensor(ab[0], aa[0], aa[1], ALU.max))
                    vec(lambda e: e.tensor_tensor(aa[0], aa[0], aa[1], ALU.min))
                    vec(lambda e: e.tensor_tensor(ab[1], aa[2], aa[3], ALU.max))
                    vec(lambda e: e.tensor_tensor(aa[2], aa[2], aa[3], ALU.min))
                    vec(lambda e: e.tensor_tensor(ab[0], ab[0], ab[1], ALU.min))
                    vec(lambda e: e.tensor_tensor(aa[0], aa[0], aa[2], ALU.max))
                    vec(lambda e: e.tensor_tensor(ab[0], ab[0], aa[0], ALU.max))
                    for hv in (h0, h1, h2v, h3):
                        act(lambda e, hv=hv: e.activation(aa[1], hv, AF.Abs))
                        vec(lambda e, hv=hv: e.tensor_tensor(aa[2], aa[1], ab[0], ALU.is_ge))
                        vec(lambda e, hv=hv: e.tensor_tensor(hv, hv, aa[2], ALU.mult))
                for bg in range(4):
                    for ibm in range(32):
                        ib8, m = ibm // 4, ibm % 4
                        ib = ih * 8 + ib8
                        dma("gpsimd", wdt[0][:, 0:512],
                            wdn_l[ib, m, :, bg * 4:bg * 4 + 4, :].rearrange("p d c -> p (d c)"))
                        hv = hview(ib8, m)
                        for dj in range(4):
                            for tt in range(2):
                                mm(lambda e, ibm=ibm, dj=dj, tt=tt, hv=hv: e.matmul(
                                    ps(dj * 2 + tt), wdt[0][:, dj * 128:dj * 128 + 128],
                                    hv[:, tt * 512:tt * 512 + 512],
                                    start=(ibm == 0), stop=(ibm == 31)))
                    for dj in range(4):
                        db = bg * 4 + dj
                        for tt in range(2):
                            a32 = acc32[:, db * 512 + tt * 0:db * 512 + 512] if False else acc32[:, (db * 2 + tt) * 256:(db * 2 + tt) * 256 + 512]
                            pass
                    for dj in range(4):
                        db = bg * 4 + dj
                        for tt in range(2):
                            a32 = acc32[:, (db * 2 + tt) * 512:(db * 2 + tt) * 512 + 512]
                            if ih == 0:
                                vec(lambda e, dj=dj, tt=tt, a32=a32: e.tensor_copy(
                                    a32, ps(dj * 2 + tt)))
                            else:
                                dma("sync", res_t[:, 0:512],
                                    src_d[db * 128:(db + 1) * 128,
                                          t0 + tt * 512:t0 + tt * 512 + 512])
                                vec(lambda e, dj=dj, tt=tt, a32=a32: e.tensor_tensor(
                                    a32, a32, ps(dj * 2 + tt), ALU.add))
                                if final_out:
                                    vec(lambda e, a32=a32: e.tensor_tensor(
                                        o32[:, 0:512], a32, res_t[:, 0:512], ALU.add))
                                    dma("sync", out_t[db * 128:(db + 1) * 128,
                                                      t0 + tt * 512:t0 + tt * 512 + 512],
                                        o32[:, 0:512])
                                else:
                                    vec(lambda e, a32=a32: e.tensor_tensor(
                                        sq_c[:, 0:512], a32, res_t[:, 0:512], ALU.add))
                                    dma("sync", dst_d[db * 128:(db + 1) * 128,
                                                      t0 + tt * 512:t0 + tt * 512 + 512],
                                        sq_c[:, 0:512])

    ATTN_PHASE = globals().get("ATTN_PHASE_LIMIT", 3)

    def attn():
        A.reset(COMMON)
        xn3 = A.take_bf16(16 * TCH)
        wqt = [A.take_bf16(16 * 128) for _ in range(2)]
        qkv_ev = A.take_bf16(TCH)
        for half in range(2):
            t0 = half * TCH
            ln_norm(x2_d, True, t0, TCH, xn3, None, extra_rms=False)
            if want("xn3"):
                for kc in range(16):
                    store(tap_ap["xn3"][kc * 128:(kc + 1) * 128, t0:t0 + TCH],
                          xn3[:, kc * TCH:(kc + 1) * TCH])
            for h in range(NH):
                for which, wl, dst in (("q", wq_l, q_d), ("k", wk_l, kag_in)):
                    dma("gpsimd", wqt[0][:], wl[h].rearrange("p k c -> p (k c)"))
                    for tt in range(2):
                        for kc in range(16):
                            mm(lambda e, tt=tt, kc=kc: e.matmul(
                                ps(0 + tt), wqt[0][:, kc * 128:kc * 128 + 128],
                                xn3[:, kc * TCH + tt * 512:kc * TCH + tt * 512 + 512],
                                start=(kc == 0), stop=(kc == 15)))
                    vec(lambda e: e.tensor_copy(qkv_ev[:, 0:512], ps(0)))
                    vec(lambda e: e.tensor_copy(qkv_ev[:, 512:1024], ps(1)))
                    dma("sync", dst[h * HN:(h + 1) * HN, t0:t0 + TCH], qkv_ev[:, 0:TCH])
                dma("gpsimd", wqt[1][:], wv_r[h].rearrange("p k c -> p (k c)"))
                for tb in range(TCH // 128):
                    for kc in range(16):
                        mm(lambda e, tb=tb, kc=kc: e.matmul(
                            ps(2, w=128), xn3[:, kc * TCH + tb * 128:kc * TCH + tb * 128 + 128],
                            wqt[1][:, kc * 128:kc * 128 + 128],
                            start=(kc == 0), stop=(kc == 15)))
                    vec(lambda e, tb=tb: e.tensor_copy(qkv_ev[:, 0:128], ps(2, w=128)))
                    dma("sync", vag_in[h * TQ + t0 + tb * 128: h * TQ + t0 + (tb + 1) * 128, :],
                        qkv_ev[:, 0:128])
        if want("qh"):
            store(tap_ap["qh"], q_d)
        P.step("gpsimd", lambda e: e.collective_compute(
            "AllGather", ALU.bypass, replica_groups=RGROUPS8,
            ins=[kag_in.opt()], outs=[kag_out.opt()]),
            incs=[(cc, 1)], collective=True)
        P.step("gpsimd", lambda e: e.collective_compute(
            "AllGather", ALU.bypass, replica_groups=RGROUPS8,
            ins=[vag_in.opt()], outs=[vag_out.opt()]),
            incs=[(cc, 1)], collective=True)
        def nrm_k(e):
            r = dyn_off(e, "sy_kv", NH * HN * TQ, 0)
            ap = bass.AP(kag_out.tensor, r, [[1, 4 * NH * HN * TQ]])
            return e.dma_start(out=kq_d.rearrange("r t -> (r t)"), in_=ap)
        P.step("sync", nrm_k, incs=[(ld, 16)], dma=True)
        def nrm_v(e):
            r = dyn_off(e, "sy_kv2", NH * TQ * HN, 0)
            ap = bass.AP(vag_out.tensor, r, [[1, 4 * NH * TQ * HN]])
            return e.dma_start(out=vq_d.rearrange("r t -> (r t)"), in_=ap)
        P.step("sync", nrm_v, incs=[(ld, 16)], dma=True)
        if ATTN_PHASE < 2:
            return
        A.reset(COMMON)
        kh_t = A.take_bf16(L)
        vh_t = [A.take_bf16(HN) for _ in range(2)]
        qh_t = A.take_bf16(TQ)
        pb_t = A.take_bf16(TCH)
        rb_t = A.take_bf16(512)
        o_ev = A.take_bf16(TQ)

        def kread(e, h, dst):
            ap = bass.AP(kq_d.tensor, h * HN * TQ, [[TQ, 128], [NH * HN * TQ, 4], [1, TQ]])
            return e.dma_start(out=dst.rearrange("p (r t) -> p r t", r=4), in_=ap)

        def vread(e, h, kt, dst):
            rank, off = kt // 16, (kt % 16) * 128
            ap = bass.AP(vq_d.tensor, (rank * NH * TQ + h * TQ + off) * HN, [[HN, 128], [1, HN]])
            return e.dma_start(out=dst, in_=ap)

        for h in range(globals().get("ATTN_NH", NH)):
            P.step("sync", lambda e, h=h: kread(e, h, kh_t[:]), incs=[(ld, 16)], dma=True)
            dma("sync", qh_t[:], q_d[h * HN:(h + 1) * HN, :])
            for qt in range(globals().get("ATTN_NQ", 4)):
                NKT = globals().get("ATTN_NKT", 64)
                for kt in range(NKT):
                    sb_ = 0 if (kt % 2 == 0) else 2
                    mm(lambda e, kt=kt, qt=qt, sb_=sb_: e.matmul(
                        ps(sb_, w=512), kh_t[:, kt * 128:kt * 128 + 128],
                        qh_t[:, qt * 512:qt * 512 + 512], start=True, stop=True))
                    act(lambda e, sb_=sb_: e.activation(
                        pb_t[:, 0:512], ps(sb_), AF.Exp, scale=1.0 / np.sqrt(HN),
                        bias=cst32[:, 0:1]))
                    P.step("sync", lambda e, h=h, kt=kt: vread(e, h, kt, vh_t[kt % 2][:, 0:HN]),
                           incs=[(ld, 16)], dma=True)
                    mm(lambda e, kt=kt: e.matmul(
                        ps(4), vh_t[kt % 2][:, 0:HN], pb_t[:, 0:512],
                        start=(kt == 0), stop=(kt == NKT - 1)))
                    mm(lambda e, kt=kt, NKT=NKT: e.matmul(
                        ps(5, nr=1), onesb[:, 0:1], pb_t[:, 0:512],
                        start=(kt == 0), stop=(kt == NKT - 1)))
                vec(lambda e, qt=qt: e.tensor_copy(rows[0:1, 0:512], ps(5, nr=1)))
                vec(lambda e: e.reciprocal(rows[0:1, 0:512], rows[0:1, 0:512]))
                vec(lambda e: e.tensor_copy(rows2[0:1, 0:512], rows[0:1, 0:512]))
                mm(lambda e: e.matmul(ps(6), onesb[0:1, 0:128], rows2[0:1, 0:512],
                                      start=True, stop=True))
                vec(lambda e: e.tensor_copy(rb_t[:, 0:512], ps(6)))
                vec(lambda e, qt=qt: e.tensor_tensor(
                    o_ev[:, qt * 512:qt * 512 + 512], ps(4), rb_t[:, 0:512], ALU.mult))
            slot = h // 2 + (h % 2) * 8
            dma("sync", q_d[slot * HN:(slot + 1) * HN, :], o_ev[:])
        if ATTN_PHASE < 3:
            return
        def ctx_stage(e):
            r = dyn_off(e, "sy_ctx_s", 1, 0, pidx=1)
            ap = bass.AP(q_d.tensor, r, [[1, 8 * HN * TQ]])
            return e.dma_start(out=agc_in.rearrange("r t -> (r t)"), in_=ap)
        P.step("sync", ctx_stage, incs=[(ld, 16)], dma=True)
        P.step("gpsimd", lambda e: e.collective_compute(
            "AllGather", ALU.bypass, replica_groups=RGROUPS8,
            ins=[agc_in.opt()], outs=[agc_out.opt()]),
            incs=[(cc, 1)], collective=True)
        def ctx_own(e):
            r = dyn_off(e, "sy_ctx_o", 1, 0, pidx=2)
            ap = bass.AP(q_d.tensor, r, [[1, 8 * HN * TQ]])
            return e.dma_start(out=ctxp_d[0:8 * HN, :].rearrange("r t -> (r t)"), in_=ap)
        P.step("sync", ctx_own, incs=[(ld, 16)], dma=True)
        def ctx_part(e):
            r = dyn_off(e, "sy_ctx_p", 1, 0, pidx=3)
            ap = bass.AP(agc_out.tensor, r, [[1, 8 * HN * TQ]])
            return e.dma_start(out=ctxp_d[8 * HN:16 * HN, :].rearrange("r t -> (r t)"), in_=ap)
        P.step("sync", ctx_part, incs=[(ld, 16)], dma=True)
        A.reset(COMMON)
        wpt = [A.take_bf16(16 * 128) for _ in range(2)]
        ctx_s = A.take_bf16(TQ)
        res_t = A.take_bf16(512)
        x3_t = A.take_bf16(512)
        for db in range(16):
            dma("gpsimd", wpt[db % 2][:], wproj_l[db].rearrange("p k c -> p (k c)"))
            for j in range(NH):
                dma("sync", ctx_s[:], ctxp_d[j * HN:(j + 1) * HN, :])
                for tt in range(4):
                    mm(lambda e, db=db, j=j, tt=tt: e.matmul(
                        ps(tt), wpt[db % 2][:, j * 128:j * 128 + 128],
                        ctx_s[:, tt * 512:tt * 512 + 512],
                        start=(j == 0), stop=(j == NH - 1)))
            for tt in range(4):
                dma("sync", res_t[:, 0:512], x2_d[db * 128:(db + 1) * 128, tt * 512:tt * 512 + 512])
                vec(lambda e, db=db, tt=tt: e.scalar_tensor_tensor(
                    x3_t[:, 0:512], ps(tt), bprj_sb[:, db:db + 1], res_t[:, 0:512],
                    ALU.add, ALU.add))
                dma("sync", x3_d[db * 128:(db + 1) * 128, tt * 512:tt * 512 + 512], x3_t[:, 0:512])
                if want("x3"):
                    store(tap_ap["x3"][db * 128:(db + 1) * 128, tt * 512:tt * 512 + 512],
                          x3_t[:, 0:512])

    order = ["mamba", "mamba_out", "ffn1", "attn", "ffn2"]
    stop_at = {"mamba": 1, "x1": 2, "ffn1": 3, "attn": 4, "all": 5}[upto]
    mamba()
    if stop_at >= 2:
        mamba_out()
    if stop_at >= 3:
        ffn(x1_d, x2_d, final_out=False)
        if want("x2"):
            for db in range(16):
                dma("sync", evtmp[:, 0:TQ], x2_d[db * 128:(db + 1) * 128, :])
                store(tap_ap["x2"][db * 128:(db + 1) * 128, :], evtmp[:, 0:TQ])
    if stop_at >= 4:
        attn()
    if stop_at >= 5:
        ffn(x3_d, None, final_out=True)
    else:
        ev32 = evtmp.bitcast(F32)
        vec(lambda e: e.memset(ev32[:, 0:TCH], 0.0))
        for db in range(16):
            for hf in range(2):
                store(out_t[db * 128:(db + 1) * 128, hf * TCH:(hf + 1) * TCH], ev32[:, 0:TCH])

    if serial:
        totals = P.chain_totals + [P._chain_tot]
        final_waits = [(chs, totals[i]) for i, chs in enumerate(P.chains)]
    else:
        final_waits = [(st, st.n), (ld, ld.n), (cc, cc.n)]
    P.step("sync", lambda e: None, waits=final_waits)
    P.emit()
    return nc, tap_ap


def kernel(**inputs):
    in_maps = stage_inputs(inputs)
    nc, _ = build(serial=True, upto="all", taps=())
    res = run_bass_kernel_spmd(nc, in_maps, core_ids=list(range(NCORE)))
    out = np.empty((B_, L, D), dtype=np.float32)
    for c in range(NCORE):
        b, q = c // 4, c % 4
        out[b, q * TQ:(q + 1) * TQ, :] = np.asarray(res.results[c]["out_t"]).T
    return out


# version 04
# speedup vs baseline: 1.5170x; 1.5170x over previous
import os
os.environ.setdefault("TRN_ALLOW_ICI_ALL_TO_ALL", "1")
import numpy as np
import ml_dtypes
import concourse.bass as bass
import concourse.mybir as mybir
from concourse.bass_utils import run_bass_kernel_spmd

F32 = mybir.dt.float32
BF16 = mybir.dt.bfloat16
I32 = mybir.dt.int32
AF = mybir.ActivationFunctionType
ALU = mybir.AluOpType
AX = mybir.AxisListType
BF = ml_dtypes.bfloat16

B_, L, D = 2, 8192, 2048
E, N_ST, DT, DC = 4096, 16, 128, 4
I_FFN = 8192
NH, HN = 16, 128
EPS = 1e-5
NCORE = 8
EC = E // 4
NEB = EC // 128
TCH = 1024
NCHUNK = L // TCH
TQ = L // 4
ATT_C = 20.0
RGROUPS4 = [[0, 1, 2, 3], [4, 5, 6, 7]]
RGROUPS8 = [list(range(8))]

ENGS = ("sync", "vector", "scalar", "tensor", "gpsimd")


class CSem:
    def __init__(self, name):
        self.h = None
        self.name = name
        self.n = 0


class Prog:

    def __init__(self, nc, serial=False):
        self.nc = nc
        self.serial = serial
        self.steps = []
        self.eng_steps = {e: [] for e in ENGS}
        self.reg = {}
        self._sems = []
        self.chains = []
        self.chain_totals = []
        self._chain_tot = 0
        self._chain_idx = -1
        self._barrier_pending = False
        if serial:
            self._new_chain()

    def _new_chain(self):
        if self.chains:
            self.chain_totals.append(self._chain_tot)
        s = self.sem(f"chain{len(self.chains)}")
        self.chains.append(s)
        self._chain_idx += 1
        self._chain_tot = 0

    def sem(self, name):
        s = CSem(name)
        self._sems.append(s)
        return s

    def barrier(self):
        self._bar = {self.eng_steps[e2][-1] for e2 in ENGS if self.eng_steps[e2]}
        self.reg = {}

    def step(self, eng, fn, reads=(), writes=(), dma=False, collective=False):
        gid = len(self.steps)
        deps = set(getattr(self, "_bar", ()))
        for r in reads:
            e = self.reg.get(r)
            if e and e["w"] is not None:
                deps.add(e["w"])
        for w in writes:
            e = self.reg.get(w)
            if e:
                if e["w"] is not None:
                    deps.add(e["w"])
                deps |= e["r"]
        for r in reads:
            self.reg.setdefault(r, {"w": None, "r": set()})["r"].add(gid)
        for w in writes:
            self.reg[w] = {"w": gid, "r": set()}
        st = {"eng": eng, "fn": fn, "amt": 16 if dma else 1, "coll": collective,
              "deps": deps, "gid": gid, "lidx": len(self.eng_steps[eng]),
              "serial_extra": None}
        if self.serial:
            if self._chain_tot > 24000:
                prev = self.chains[self._chain_idx]
                prev_tot = self._chain_tot
                self._new_chain()
                st["serial_extra"] = (prev, prev_tot)
            st["chain"] = (self.chains[self._chain_idx], self._chain_tot)
            amt = 1 if collective else (16 if dma else 1)
            st["chain_inc"] = (self.chains[self._chain_idx], amt)
            self._chain_tot += amt
        self.steps.append(st)
        self.eng_steps[eng].append(gid)
        return gid

    def emit(self):
        import contextlib
        nc = self.nc
        for st in self.steps:
            st["deps"] = {d for d in st["deps"] if self.steps[d]["eng"] != st["eng"]}
        flagged = set()
        for st in self.steps:
            flagged |= st["deps"]
        for st in self.steps:
            if st["amt"] == 16 or st["coll"]:
                flagged.add(st["gid"])
        SEMCAP = 30000
        cum = {}
        self.prog_sem = {}
        for e in ENGS:
            si = 0
            sem = self.sem(f"prog_{e}_0")
            tot = 0
            for g in self.eng_steps[e]:
                if g in flagged:
                    amt = 1 if self.steps[g]["coll"] else self.steps[g]["amt"]
                    if tot + amt > SEMCAP:
                        si += 1
                        sem = self.sem(f"prog_{e}_{si}")
                        tot = 0
                    tot += amt
                    self.prog_sem[g] = sem
                cum[g] = (sem, tot)
        for st in self.steps:
            w = {}
            for d in st["deps"]:
                sem, v = cum[d]
                w[sem] = max(w.get(sem, 0), v)
            st["waits"] = [(sem, v) for sem, v in w.items() if v > 0]

        with contextlib.ExitStack() as stack:
            for s in self._sems:
                s.h = stack.enter_context(nc.semaphore(s.name))
            block = stack.enter_context(nc.Block())

            def runner(name):
                def _run(e):
                    for g in self.eng_steps[name]:
                        st = self.steps[g]
                        if st.get("serial_extra"):
                            ps_, pv_ = st["serial_extra"]
                            e.wait_ge(ps_.h, pv_)
                        if self.serial:
                            cs, cv = st["chain"]
                            e.wait_ge(cs.h, cv)
                        for (sm, v) in st["waits"]:
                            e.wait_ge(sm.h, v)
                        inst = st["fn"](e)
                        incs = []
                        if self.serial:
                            incs.append(st["chain_inc"])
                        if g in flagged:
                            amt = 1 if st["coll"] else st["amt"]
                            incs.append((self.prog_sem[g], amt))
                        if inst is None:
                            for (sm, a) in incs:
                                e.sem_inc(sm.h, a)
                            continue
                        if incs:
                            if st["coll"]:
                                inst.then_inc(incs[0][0].h)
                            else:
                                inst.then_inc(incs[0][0].h, incs[0][1])
                            for (sm, a) in incs[1:]:
                                e.sem_inc(sm.h, a)
                return _run

            block.sync(runner("sync"))
            block.vector(runner("vector"))
            block.scalar(runner("scalar"))
            block.tensor(runner("tensor"))
            block.gpsimd(runner("gpsimd"))


class Carve:
    def __init__(self, t, cols, name=""):
        self.t, self.cols, self.pos, self.name = t, cols, 0, name

    def take_bf16(self, n):
        n_al = (n + 15) // 16 * 16
        assert self.pos + n_al <= self.cols, f"arena {self.name}: {self.pos}+{n_al}>{self.cols}"
        ap = self.t[:, self.pos:self.pos + n]
        self.pos += n_al
        return ap

    def take_f32(self, n):
        ap = self.take_bf16(2 * n)
        return ap.bitcast(F32)

    def mark(self):
        return self.pos

    def reset(self, pos=0):
        self.pos = pos


def _pack_lhsT(A):
    Ktot, Mtot = A.shape
    mt = max(Mtot // 128, 1)
    mcols = Mtot // mt
    return np.ascontiguousarray(A.reshape(Ktot // 128, 128, mt, mcols).transpose(2, 1, 0, 3))


def stage_inputs(inp):
    f = np.float32
    g = lambda k: np.asarray(inp[k], f)
    x = g("L0_attn_a_x")
    Wskip, Win = g("L0_attn_a_W_skip"), g("L0_attn_a_W_in")
    convw, convb = g("L0_attn_a_conv_w"), g("L0_attn_a_conv_b")
    Wd1, Wd2, bd2 = g("L0_attn_a_Wd1"), g("L0_attn_a_Wd2"), g("L0_attn_a_bd2")
    WBm, WCm, Alog, WD = g("L0_attn_a_WB"), g("L0_attn_a_WC"), g("L0_attn_a_A_log"), g("L0_attn_a_W_D")
    Wout, normw = g("L0_attn_a_W_out"), g("L0_attn_a_norm_w")
    Wqkv, Wproj, bproj = g("L0_attn_b_Wqkv"), g("L0_attn_b_Wproj"), g("L0_attn_b_bproj")
    wg, wu, wdn = g("L0_ffn_w_gate"), g("L0_ffn_w_up"), g("L0_ffn_w_down")

    xt = [np.ascontiguousarray(x[b].T) for b in range(B_)]
    wout_l = _pack_lhsT(Wout.T)
    normw_t = np.ascontiguousarray(normw.reshape(16, 128).T)
    wg_l = np.ascontiguousarray(wg.reshape(16, 128, 16, 128, 4).transpose(2, 4, 1, 0, 3))
    wu_l = np.ascontiguousarray(wu.reshape(16, 128, 16, 128, 4).transpose(2, 4, 1, 0, 3))
    wdn_l = np.ascontiguousarray(wdn.reshape(16, 128, 4, 16, 128).transpose(0, 2, 1, 3, 4))
    qrows = np.concatenate([Wqkv[h * 384:h * 384 + 128] for h in range(NH)], 0)
    krows = np.concatenate([Wqkv[h * 384 + 128:h * 384 + 256] for h in range(NH)], 0)
    vrows = np.concatenate([Wqkv[h * 384 + 256:h * 384 + 384] for h in range(NH)], 0)
    _qk = lambda Wr: np.ascontiguousarray(Wr.reshape(NH, 128, 16, 128).transpose(0, 3, 2, 1))
    wq_l, wk_l, wv_r = _qk(qrows), _qk(krows), _qk(vrows)
    wproj_l0 = _pack_lhsT(Wproj.T)
    wproj_lb = [wproj_l0,
                np.ascontiguousarray(wproj_l0[:, :, list(range(8, 16)) + list(range(8)), :])]
    bproj_t = np.ascontiguousarray(bproj.reshape(16, 128).T)

    in_maps = []
    for c in range(NCORE):
        b, sh = c // 4, c % 4
        es = slice(sh * EC, (sh + 1) * EC)
        in_maps.append({
            "xt": xt[b],
            "xqt": np.ascontiguousarray(x[b, sh * TQ:(sh + 1) * TQ, :].T),
            "wskip_l": _pack_lhsT(Wskip[es].T),
            "win_l": _pack_lhsT(Win[es].T),
            "convw": np.ascontiguousarray(convw[es, 0, :].reshape(NEB, 128, DC)),
            "convb_t": np.ascontiguousarray(convb[es].reshape(NEB, 128).T),
            "bd2_t": np.ascontiguousarray(bd2[es].reshape(NEB, 128).T),
            "wd_t": np.ascontiguousarray(WD[es].reshape(NEB, 128).T),
            "wd1_l": np.ascontiguousarray(_pack_lhsT(Wd1.T[es]).reshape(128, NEB, 128)),
            "wd2_l": np.ascontiguousarray(_pack_lhsT(Wd2[es].T).reshape(NEB, 128, 128)),
            "wbc_l": np.ascontiguousarray(
                _pack_lhsT(np.concatenate([WBm, WCm], 0).T[es]).reshape(128, NEB, 32)),
            "alog": np.ascontiguousarray(Alog[es].reshape(NEB, 128, N_ST)),
            "wout_l": wout_l,
            "normw_t": normw_t,
            "wg_l": wg_l, "wu_l": wu_l, "wdn_l": wdn_l,
            "wq_l": wq_l, "wk_l": wk_l, "wv_r": wv_r,
            "wproj_l": wproj_lb[c // 4], "bproj_t": bproj_t,
            "params": np.array([4 * (c // 4), (1 - c // 4) * 8 * HN * TQ,
                                (c // 4) * 8 * HN * TQ, (c ^ 4) * 8 * HN * TQ,
                                0, 0, 0, 0], dtype=np.int32),
        })
    return in_maps


def build(serial=True, upto="all", taps=()):
    nc = bass.Bass("TRN2", target_bir_lowering=False)
    want = lambda t: t in taps

    din = lambda name, shape, dt=F32: nc.dram_tensor(name, shape, dt, kind="ExternalInput").ap()
    xt = din("xt", [D, L])
    xqt = din("xqt", [D, TQ])
    wskip_l = din("wskip_l", [NEB, 128, 16, 128])
    win_l = din("win_l", [NEB, 128, 16, 128])
    convw = din("convw", [NEB, 128, DC])
    convb_t = din("convb_t", [128, NEB])
    bd2_t = din("bd2_t", [128, NEB])
    wd_t = din("wd_t", [128, NEB])
    wd1_l = din("wd1_l", [128, NEB, 128])
    wd2_l = din("wd2_l", [NEB, 128, 128])
    wbc_l = din("wbc_l", [128, NEB, 32])
    alog = din("alog", [NEB, 128, N_ST])
    wout_l = din("wout_l", [16, 128, 32, 128])
    normw_t = din("normw_t", [128, 16])
    wg_l = din("wg_l", [16, 4, 128, 16, 128])
    wu_l = din("wu_l", [16, 4, 128, 16, 128])
    wdn_l = din("wdn_l", [16, 4, 128, 16, 128])
    wq_l = din("wq_l", [NH, 128, 16, 128])
    wk_l = din("wk_l", [NH, 128, 16, 128])
    wv_r = din("wv_r", [NH, 128, 16, 128])
    wproj_l = din("wproj_l", [16, 128, 16, 128])
    bproj_t = din("bproj_t", [128, 16])
    params = din("params", [8], I32)

    out_t = nc.dram_tensor("out_t", [D, TQ], F32, kind="ExternalOutput").ap()

    tap_ap = {}
    def mktap(name, shape, dt=F32):
        if want(name):
            tap_ap[name] = nc.dram_tensor("tap_" + name, shape, dt, kind="ExternalOutput").ap()
    mktap("xn", [NCHUNK, D, TCH], BF16)
    mktap("u", [NCHUNK, EC, TCH], BF16)
    mktap("sskip", [NCHUNK, EC, TCH], BF16)
    mktap("delta", [NCHUNK, EC, TCH], BF16)
    mktap("bc", [NCHUNK, 32, TCH])
    mktap("arin", [NCHUNK, 160, TCH])
    mktap("arout", [NCHUNK, 160, TCH])
    mktap("y", [NCHUNK, EC, TCH], BF16)
    mktap("yt", [NCHUNK, EC, TCH], BF16)
    mktap("x1", [D, TQ], BF16)
    mktap("h_ffn", [I_FFN, TQ], BF16)
    mktap("hs_ffn", [I_FFN, TQ], BF16)
    mktap("x2", [D, TQ], BF16)
    mktap("xn3", [D, TQ], BF16)
    mktap("x3", [D, TQ], BF16)

    dscr = lambda name, shape, dt: nc.dram_tensor(name, shape, dt, kind="Internal").ap()
    sskip_d = dscr("sskip_d", [NEB, 128, L], BF16)
    ar_in = dscr("ar_in", [NCHUNK, 160, TCH], F32)
    ar_out = dscr("ar_out", [NCHUNK, 160, TCH], F32)
    a2a_in = dscr("a2a_in", [8 * EC, TQ], BF16)
    a2a_out = dscr("a2a_out", [8 * EC, TQ], BF16)
    x1_d = dscr("x1_d", [D, TQ], BF16)
    x2_d = dscr("x2_d", [D, TQ], BF16)
    x3_d = dscr("x3_d", [D, TQ], BF16)
    kag_in = dscr("kag_in", [NH * HN, TQ], BF16)
    kag_out = dscr("kag_out", [8 * NH * HN, TQ], BF16)
    vag_in = dscr("vag_in", [NH * TQ, HN], BF16)
    vag_out = dscr("vag_out", [8 * NH * TQ, HN], BF16)
    q_d = dscr("q_d", [NH * HN, TQ], BF16)
    a2a_stage = dscr("a2a_stage", [4 * EC, TQ], BF16)
    ytq_d = dscr("ytq_d", [4 * EC, TQ], BF16)
    kq_d = dscr("kq_d", [4 * NH * HN, TQ], BF16)
    vq_d = dscr("vq_d", [4 * NH * TQ, HN], BF16)
    agc_in = dscr("agc_in", [8 * HN, TQ], BF16)
    agc_out = dscr("agc_out", [8 * 8 * HN, TQ], BF16)
    ctxp_d = dscr("ctxp_d", [16 * HN, TQ], BF16)

    AR_COLS = 96_256
    arena_t = nc.alloc_sbuf_tensor("arena", [128, AR_COLS], BF16)
    A = Carve(arena_t, AR_COLS, "A")

    consts = nc.alloc_sbuf_tensor("consts", [128, 256], BF16)
    onesb = consts[:, 0:128]
    nw_sb = nc.alloc_sbuf_tensor("nw_sb", [128, 16], F32)
    cvb_sb = nc.alloc_sbuf_tensor("cvb_sb", [128, NEB], F32)
    bd2_sb = nc.alloc_sbuf_tensor("bd2_sb", [128, NEB], F32)
    wdcol_sb = nc.alloc_sbuf_tensor("wdcol_sb", [128, NEB], F32)
    cw_sb = nc.alloc_sbuf_tensor("cw_sb", [128, NEB * DC], F32)
    a_sb = nc.alloc_sbuf_tensor("a_sb", [128, NEB * N_ST], F32)
    state_sb = nc.alloc_sbuf_tensor("state_sb", [128, NEB * N_ST], F32)
    halo_sb = nc.alloc_sbuf_tensor("halo_sb", [128, NEB * (DC - 1)], BF16)
    rows = nc.alloc_sbuf_tensor("rows", [1, 3 * TCH], F32)
    rows2 = nc.alloc_sbuf_tensor("rows2", [1, 2 * TCH], BF16)
    psel = nc.alloc_sbuf_tensor("psel", [1, 8], I32)
    bprj_sb = nc.alloc_sbuf_tensor("bprj_sb", [128, 16], F32)
    cst32 = nc.alloc_sbuf_tensor("cst32", [128, 2], F32)

    PS = nc.alloc_psum_tensor("PS", [128, 4096], F32)
    def ps(i, w=512, r0=0, nr=128):
        return PS[r0:r0 + nr, 512 * i:512 * i + w]

    P = Prog(nc, serial=serial)
    st = P.sem("st")
    ld = P.sem("ld")
    cc = P.sem("cc")

    mu_b = A.take_bf16(TCH)
    s_b = A.take_bf16(TCH)
    sq_c = A.take_bf16(TCH)
    evtmp = A.take_bf16(2 * TCH)
    COMMON = A.mark()

    def dma(eng, out, in_, r=(), w=(), **kw):
        assert int(np.prod(out.shape)) == int(np.prod(in_.shape)), \
            f"dma size mismatch: out{tuple(out.shape)} in{tuple(in_.shape)}"
        P.step(eng, lambda e, out=out, in_=in_, kw=kw: e.dma_start(out=out, in_=in_, **kw),
               reads=r, writes=w, dma=True)

    def store(out, in_, r=(), eng="sync"):
        P.step(eng, lambda e, out=out, in_=in_: e.dma_start(out=out, in_=in_),
               reads=r, writes=("OUT",), dma=True)

    def vec(fn, r=(), w=()):
        P.step("vector", fn, reads=r, writes=w)

    def pool(fn, r=(), w=()):
        P.step("gpsimd", fn, reads=r, writes=w)

    def act(fn, r=(), w=()):
        P.step("scalar", fn, reads=r, writes=w)

    def mm(fn, r=(), w=()):
        P.step("tensor", fn, reads=r, writes=w)

    regs = {}
    def dyn_off(e, key, mul, base, pidx=0):
        if key not in regs:
            rb = e.alloc_register(key + "_b")
            e.reg_load(rb, psel[0:1, pidx:pidx + 1])
            e.reg_mul(rb, rb, mul)
            ra = e.alloc_register(key + "_a")
            regs[key] = (rb, ra)
        rb, ra = regs[key]
        e.reg_add(ra, rb, base)
        return ra

    dma("sync", nw_sb[:], normw_t, w=["nw"])
    dma("sync", cvb_sb[:], convb_t, w=["cvb"])
    dma("sync", bd2_sb[:], bd2_t, w=["bd2"])
    dma("sync", wdcol_sb[:], wd_t, w=["wdc"])
    dma("sync", cw_sb[:].rearrange("p (b c) -> p b c", b=NEB), convw.rearrange("b p c -> p b c"), w=["cw"])
    dma("sync", bprj_sb[:], bproj_t, w=["bprj"])
    dma("sync", psel[:], params[None, :], w=["psel"])
    dma("sync", a_sb[:].rearrange("p (b n) -> p b n", b=NEB), alog.rearrange("b p n -> p b n"), w=["a"])
    vec(lambda e: e.memset(consts[:, :], 0.0), w=["consts"])
    vec(lambda e: e.memset(onesb, 1.0), r=["consts"], w=["consts"])
    act(lambda e: e.activation(a_sb[:], a_sb[:], AF.Exp), r=["a"], w=["a"])
    vec(lambda e: e.tensor_scalar(a_sb[:], a_sb[:], -1.0, None, ALU.mult), r=["a"], w=["a"])
    vec(lambda e: e.memset(cst32[:, 0:1], -ATT_C), w=["cst32"])
    vec(lambda e: e.memset(state_sb[:], 0.0), w=["state"])
    vec(lambda e: e.memset(halo_sb[:], 0.0), w=["halo"])

    xb_c = A.take_bf16(16 * TCH)
    u_c = [A.take_bf16(NEB * TCH) for _ in range(2)]
    dl_c = A.take_bf16(NEB * TCH)
    upre_c = A.take_bf16(NEB * TCH)
    d1ar_c = A.take_bf16(TCH)
    bb_c = A.take_bf16(2 * TCH)
    cbx_c = A.take_bf16(2 * TCH)
    dA_c = A.take_bf16(2 * TCH)
    d1t_c = A.take_bf16(2 * TCH)
    h_c = A.take_bf16(2 * TCH)
    w_c = A.take_bf16(TCH)
    yac_c = A.take_bf16(TCH)
    wt = [A.take_bf16(16 * 128) for _ in range(4)]
    sk_c = A.take_bf16(TCH)
    mamba_mark = A.mark()

    def ln_norm(src_dram, src_is_bf16, tok0, T, xb_dst, gamma_col, extra_rms,
                src_key=None, xb_key="xb"):
        ev32 = evtmp.bitcast(F32)
        srcr = [src_key] if src_key else []
        XK = xb_key
        for kc in range(16):
            xv = xb_dst[:, kc * T:(kc + 1) * T]
            if src_is_bf16:
                dma("sync", xv, src_dram[kc * 128:(kc + 1) * 128, tok0:tok0 + T],
                    r=srcr, w=[XK])
            else:
                dma("sync", ev32[:, 0:T], src_dram[kc * 128:(kc + 1) * 128, tok0:tok0 + T],
                    r=srcr, w=["ev"])
                vec(lambda e, xv=xv: e.tensor_copy(xv, ev32[:, 0:T]), r=["ev"], w=[XK])
            act(lambda e, xv=xv: e.activation(sq_c[:, 0:T], xv, AF.Square), r=[XK], w=["sq"])
            for h2 in range(T // 512):
                mm(lambda e, kc=kc, h2=h2, xv=xv: e.matmul(
                    ps(4 + h2, nr=1), onesb[:, 0:1], xv[:, h2 * 512:h2 * 512 + 512],
                    start=(kc == 0), stop=(kc == 15)), r=[XK, "consts"], w=[f"p{4 + h2}"])
                mm(lambda e, kc=kc, h2=h2: e.matmul(
                    ps(6 + h2, nr=1), onesb[:, 0:1], sq_c[:, h2 * 512:h2 * 512 + 512],
                    start=(kc == 0), stop=(kc == 15)), r=["sq", "consts"], w=[f"p{6 + h2}"])
        r_mu, r_v, r_t = (rows[0:1, i * TCH:i * TCH + T] for i in range(3))
        r_s = r_v
        vec(lambda e: e.tensor_scalar(r_mu, PS[0:1, 2048:2048 + T], 1.0 / D, None, ALU.mult),
            r=["p4", "p5"], w=["rows"])
        vec(lambda e: e.tensor_scalar(r_v, PS[0:1, 3072:3072 + T], 1.0 / D, None, ALU.mult),
            r=["p6", "p7"], w=["rows"])
        vec(lambda e: e.tensor_tensor(r_t, r_mu, r_mu, ALU.mult), r=["rows"], w=["rows"])
        vec(lambda e: e.tensor_tensor(r_v, r_v, r_t, ALU.subtract), r=["rows"], w=["rows"])
        vec(lambda e: e.tensor_scalar(r_t, r_v, EPS, None, ALU.add), r=["rows"], w=["rows"])
        vec(lambda e: e.reciprocal(r_t, r_t), r=["rows"], w=["rows"])
        if extra_rms:
            vec(lambda e: e.tensor_tensor(r_v, r_v, r_t, ALU.mult), r=["rows"], w=["rows"])
            vec(lambda e: e.tensor_scalar(r_v, r_v, EPS, None, ALU.add), r=["rows"], w=["rows"])
            vec(lambda e: e.reciprocal(r_v, r_v), r=["rows"], w=["rows"])
            vec(lambda e: e.tensor_tensor(r_v, r_v, r_t, ALU.mult), r=["rows"], w=["rows"])
            act(lambda e: e.activation(r_s, r_v, AF.Sqrt), r=["rows"], w=["rows"])
        else:
            act(lambda e: e.activation(r_s, r_t, AF.Sqrt), r=["rows"], w=["rows"])
        vec(lambda e: e.tensor_copy(rows2[0:1, 0:T], r_mu), r=["rows"], w=["rows2"])
        vec(lambda e: e.tensor_copy(rows2[0:1, TCH:TCH + T], r_s), r=["rows"], w=["rows2"])
        for h2 in range(T // 512):
            mm(lambda e, h2=h2: e.matmul(ps(4 + h2), onesb[0:1, 0:128],
                                         rows2[0:1, h2 * 512:h2 * 512 + 512],
                                         start=True, stop=True),
               r=["rows2", "consts"], w=[f"p{4 + h2}"])
            mm(lambda e, h2=h2: e.matmul(ps(6 + h2), onesb[0:1, 0:128],
                                         rows2[0:1, TCH + h2 * 512:TCH + h2 * 512 + 512],
                                         start=True, stop=True),
               r=["rows2", "consts"], w=[f"p{6 + h2}"])
        vec(lambda e: e.tensor_copy(mu_b[:, 0:T], PS[:, 2048:2048 + T]), r=["p4", "p5"], w=["mu"])
        vec(lambda e: e.tensor_copy(s_b[:, 0:T], PS[:, 3072:3072 + T]), r=["p6", "p7"], w=["sb"])
        for kc in range(16):
            xv = xb_dst[:, kc * T:(kc + 1) * T]
            vec(lambda e, xv=xv: e.tensor_tensor(xv, xv, mu_b[:, 0:T], ALU.subtract),
                r=[XK, "mu"], w=[XK])
            if gamma_col is not None:
                vec(lambda e, xv=xv, kc=kc: e.scalar_tensor_tensor(
                    xv, xv, gamma_col[:, kc:kc + 1], s_b[:, 0:T], ALU.mult, ALU.mult),
                    r=[XK, "nw", "sb"], w=[XK])
            else:
                vec(lambda e, xv=xv: e.tensor_tensor(xv, xv, s_b[:, 0:T], ALU.mult),
                    r=[XK, "sb"], w=[XK])

    def mamba():
        for c in range(NCHUNK):
            uc = u_c[c % 2]
            UK = f"u{c % 2}"
            t0 = c * TCH
            ln_norm(xt, False, t0, TCH, xb_c, nw_sb, extra_rms=True)
            if want("xn"):
                store(tap_ap["xn"][c].rearrange("(k p) t -> p k t", p=128), xb_c[:].rearrange("p (k t) -> p k t", k=16), r=["xb"])
            for eb in range(NEB):
                dma("gpsimd", wt[0][:], wskip_l[eb].rearrange("p k c -> p (k c)"), w=["wt0"])
                dma("gpsimd", wt[1][:], win_l[eb].rearrange("p k c -> p (k c)"), w=["wt1"])
                for tt in range(2):
                    for kc in range(16):
                        mm(lambda e, tt=tt, kc=kc: e.matmul(
                            ps(0 + tt), wt[0][:, kc * 128:kc * 128 + 128],
                            xb_c[:, kc * TCH + tt * 512:kc * TCH + tt * 512 + 512],
                            start=(kc == 0), stop=(kc == 15)),
                           r=["wt0", "xb"], w=[f"p{tt}"])
                act(lambda e: e.activation(sk_c[:, 0:512], ps(0), AF.Silu), r=["p0"], w=["sk"])
                act(lambda e: e.activation(sk_c[:, 512:1024], ps(1), AF.Silu), r=["p1"], w=["sk"])
                dma("sync", sskip_d[eb, :, t0:t0 + TCH], sk_c[:, 0:TCH], r=["sk"], w=["Dskip"])
                if want("sskip"):
                    store(tap_ap["sskip"][c, eb * 128:(eb + 1) * 128, :], sk_c[:, 0:TCH], r=["sk"])
                for tt in range(2):
                    for kc in range(16):
                        mm(lambda e, tt=tt, kc=kc: e.matmul(
                            ps(2 + tt), wt[1][:, kc * 128:kc * 128 + 128],
                            xb_c[:, kc * TCH + tt * 512:kc * TCH + tt * 512 + 512],
                            start=(kc == 0), stop=(kc == 15)),
                           r=["wt1", "xb"], w=[f"p{2 + tt}"])
                vec(lambda e, eb=eb: e.tensor_copy(upre_c[:, eb * TCH:eb * TCH + 512], ps(2)),
                    r=["p2"], w=["upre"])
                vec(lambda e, eb=eb: e.tensor_copy(upre_c[:, eb * TCH + 512:eb * TCH + 1024], ps(3)),
                    r=["p3"], w=["upre"])
            for eb in range(NEB):
                up = upre_c[:, eb * TCH:(eb + 1) * TCH]
                uv = uc[:, eb * TCH:(eb + 1) * TCH]
                hl = halo_sb[:, eb * 3:eb * 3 + 3]
                vec(lambda e, up=up, uv=uv, eb=eb: e.tensor_scalar(
                    uv, up, cw_sb[:, eb * DC + 3:eb * DC + 4], None, ALU.mult),
                    r=["upre", "cw"], w=[UK])
                for j in range(3):
                    s = 3 - j
                    vec(lambda e, up=up, uv=uv, eb=eb, j=j, s=s: e.scalar_tensor_tensor(
                        uv[:, s:TCH], up[:, 0:TCH - s], cw_sb[:, eb * DC + j:eb * DC + j + 1],
                        uv[:, s:TCH], ALU.mult, ALU.add),
                        r=["upre", "cw", UK], w=[UK])
                    for bcol in range(s):
                        hidx = 3 + bcol - s
                        vec(lambda e, uv=uv, hl=hl, eb=eb, j=j, bcol=bcol, hidx=hidx:
                            e.scalar_tensor_tensor(
                                uv[:, bcol:bcol + 1], hl[:, hidx:hidx + 1],
                                cw_sb[:, eb * DC + j:eb * DC + j + 1],
                                uv[:, bcol:bcol + 1], ALU.mult, ALU.add),
                            r=["halo", "cw", UK], w=[UK])
                vec(lambda e, up=up, hl=hl: e.tensor_copy(hl, up[:, TCH - 3:TCH]),
                    r=["upre"], w=["halo"])
                act(lambda e, uv=uv, eb=eb: e.activation(uv, uv, AF.Silu,
                                                         bias=cvb_sb[:, eb:eb + 1]),
                    r=[UK, "cvb"], w=[UK])
            if want("u"):
                for eb in range(NEB):
                    store(tap_ap["u"][c, eb * 128:(eb + 1) * 128, :],
                          uc[:, eb * TCH:(eb + 1) * TCH], r=[UK])
            dma("gpsimd", wt[2][:, 0:NEB * 128], wd1_l.rearrange("p b c -> p (b c)"), w=["wt2"])
            dma("gpsimd", wt[3][:, 0:NEB * 32], wbc_l.rearrange("p b c -> p (b c)"), w=["wt3"])
            for tt in range(2):
                for eb in range(NEB):
                    mm(lambda e, tt=tt, eb=eb, uc=uc: e.matmul(
                        ps(0 + tt), wt[2][:, eb * 128:eb * 128 + 128],
                        uc[:, eb * TCH + tt * 512:eb * TCH + tt * 512 + 512],
                        start=(eb == 0), stop=(eb == NEB - 1)),
                       r=["wt2", UK], w=[f"p{tt}"])
                for eb in range(NEB):
                    mm(lambda e, tt=tt, eb=eb, uc=uc: e.matmul(
                        ps(2 + tt, nr=32), wt[3][:, eb * 32:eb * 32 + 32],
                        uc[:, eb * TCH + tt * 512:eb * TCH + tt * 512 + 512],
                        start=(eb == 0), stop=(eb == NEB - 1)),
                       r=["wt3", UK], w=[f"p{2 + tt}"])
            ev32 = evtmp.bitcast(F32)
            vec(lambda e: e.tensor_copy(ev32[:, 0:512], ps(0)), r=["p0"], w=["ev"])
            vec(lambda e: e.tensor_copy(ev32[:, 512:1024], ps(1)), r=["p1"], w=["ev"])
            dma("sync", ar_in[c, 0:128, :], ev32[:, 0:TCH], r=["ev"], w=[f"Dari{c}"])
            vec(lambda e: e.tensor_copy(ev32[0:32, 0:512], ps(2, nr=32)), r=["p2", "ev"], w=["ev2"])
            vec(lambda e: e.tensor_copy(ev32[0:32, 512:1024], ps(3, nr=32)), r=["p3", "ev"], w=["ev2"])
            dma("sync", ar_in[c, 128:160, :], ev32[0:32, 0:TCH], r=["ev2"], w=[f"Dari{c}"])
            if want("arin"):
                store(tap_ap["arin"][c], ar_in[c], r=[f"Dari{c}"])
            P.step("gpsimd", lambda e, c=c: e.collective_compute(
                "AllReduce", ALU.add, replica_groups=RGROUPS4,
                ins=[ar_in[c].opt()], outs=[ar_out[c].opt()]),
                reads=[f"Dari{c}"], writes=[f"Daro{c}"], collective=True)
            if want("bc"):
                store(tap_ap["bc"][c], ar_out[c, 128:160, :], r=[f"Daro{c}"])
            if want("arout"):
                store(tap_ap["arout"][c], ar_out[c], r=[f"Daro{c}"])
            dma("gpsimd", d1ar_c[:], ar_out[c, 0:128, :], r=[f"Daro{c}"], w=["d1ar"])
            dma("gpsimd", wt[2][:, 0:NEB * 128].rearrange("p (b c) -> p b c", b=NEB), wd2_l.rearrange("b p c -> p b c"), w=["wt2"])
            for eb in range(NEB):
                for tt in range(2):
                    mm(lambda e, eb=eb, tt=tt: e.matmul(
                        ps(4 + tt), wt[2][:, eb * 128:eb * 128 + 128],
                        d1ar_c[:, tt * 512:tt * 512 + 512], start=True, stop=True),
                       r=["wt2", "d1ar"], w=[f"p{4 + tt}"])
                dv = dl_c[:, eb * TCH:(eb + 1) * TCH]
                act(lambda e, dv=dv, eb=eb: e.activation(
                    dv[:, 0:512], ps(4), AF.Exp, bias=bd2_sb[:, eb:eb + 1]),
                    r=["p4", "bd2"], w=["dl"])
                act(lambda e, dv=dv, eb=eb: e.activation(
                    dv[:, 512:1024], ps(5), AF.Exp, bias=bd2_sb[:, eb:eb + 1]),
                    r=["p5", "bd2"], w=["dl"])
                act(lambda e, dv=dv: e.activation(dv, dv, AF.Ln, bias=1.0), r=["dl"], w=["dl"])
            if want("delta"):
                for eb in range(NEB):
                    store(tap_ap["delta"][c, eb * 128:(eb + 1) * 128, :],
                          dl_c[:, eb * TCH:(eb + 1) * TCH], r=["dl"])
            for eb in range(NEB):
                dv = dl_c[:, eb * TCH:(eb + 1) * TCH]
                uv = uc[:, eb * TCH:(eb + 1) * TCH]
                vec(lambda e, dv=dv, uv=uv: e.tensor_tensor(w_c[:, 0:TCH], dv, uv, ALU.mult),
                    r=["dl", UK], w=["w"])
                for n in range(N_ST):
                    r = n % 2
                    bb = bb_c[:, r * TCH:(r + 1) * TCH]
                    cbv = cbx_c[:, r * TCH:(r + 1) * TCH]
                    dav = dA_c[:, r * TCH:(r + 1) * TCH]
                    d1v = d1t_c[:, r * TCH:(r + 1) * TCH]
                    hv = h_c[:, r * TCH:(r + 1) * TCH]
                    dma("gpsimd", bb, ar_out[c, 128 + n:129 + n, :].partition_broadcast(128),
                        r=[f"Daro{c}"], w=[f"bb{r}"])
                    dma("gpsimd", cbv, ar_out[c, 144 + n:145 + n, :].partition_broadcast(128),
                        r=[f"Daro{c}"], w=[f"cb{r}"])
                    act(lambda e, dav=dav, dv=dv, eb=eb, n=n: e.activation(
                        dav, dv, AF.Exp, scale=a_sb[:, eb * N_ST + n:eb * N_ST + n + 1]),
                        r=["dl", "a"], w=[f"dA{r}"])
                    pool(lambda e, d1v=d1v, bb=bb: e.tensor_tensor(
                        d1v, w_c[:, 0:TCH], bb, ALU.mult),
                        r=["w", f"bb{r}"], w=[f"d1t{r}"])
                    sidx = eb * N_ST + n
                    vec(lambda e, hv=hv, dav=dav, d1v=d1v, sidx=sidx: e.tensor_tensor_scan(
                        hv, dav, d1v, state_sb[:, sidx:sidx + 1], ALU.mult, ALU.add),
                        r=[f"dA{r}", f"d1t{r}", "state"], w=[f"h{r}"])
                    vec(lambda e, hv=hv, sidx=sidx: e.tensor_copy(
                        state_sb[:, sidx:sidx + 1], hv[:, TCH - 1:TCH]),
                        r=[f"h{r}"], w=["state"])
                    if n == 0:
                        vec(lambda e, hv=hv, cbv=cbv: e.tensor_tensor(
                            yac_c[:, 0:TCH], hv, cbv, ALU.mult),
                            r=[f"h{r}", f"cb{r}"], w=["yac"])
                    else:
                        vec(lambda e, hv=hv, cbv=cbv, r=r: e.tensor_tensor(
                            evtmp[:, r * TCH:(r + 1) * TCH], hv, cbv, ALU.mult),
                            r=[f"h{r}", f"cb{r}"], w=["ev"])
                        vec(lambda e, r=r: e.tensor_tensor(
                            yac_c[:, 0:TCH], yac_c[:, 0:TCH],
                            evtmp[:, r * TCH:(r + 1) * TCH], ALU.add),
                            r=["yac", "ev"], w=["yac"])
                if want("y"):
                    store(tap_ap["y"][c, eb * 128:(eb + 1) * 128, :], yac_c[:, 0:TCH], r=["yac"])
                dma("sync", sk_c[:, 0:TCH], sskip_d[eb, :, t0:t0 + TCH], r=["Dskip"], w=["sk"])
                vec(lambda e, uv=uv, eb=eb: e.scalar_tensor_tensor(
                    yac_c[:, 0:TCH], uv, wdcol_sb[:, eb:eb + 1], yac_c[:, 0:TCH],
                    ALU.mult, ALU.add),
                    r=[UK, "wdc", "yac"], w=["yac"])
                vec(lambda e: e.tensor_tensor(yac_c[:, 0:TCH], yac_c[:, 0:TCH],
                                              sk_c[:, 0:TCH], ALU.mult),
                    r=["yac", "sk"], w=["yac"])
                if want("yt"):
                    store(tap_ap["yt"][c, eb * 128:(eb + 1) * 128, :], yac_c[:, 0:TCH], r=["yac"])
                dest_block = c // 2
                col0 = (c % 2) * TCH
                row0 = dest_block * EC + eb * 128
                dma("sync", a2a_stage[row0:row0 + 128, col0:col0 + TCH], yac_c[:, 0:TCH],
                    r=["yac"], w=["Dstg"])

    def mamba_out():
        def nrm_in(e):
            r = dyn_off(e, "sy_a2a", EC * TQ, 0)
            ap = bass.AP(a2a_in.tensor, r, [[1, 4 * EC * TQ]])
            return e.dma_start(out=ap, in_=a2a_stage.rearrange("r t -> (r t)"))
        P.step("sync", nrm_in, reads=["Dstg", "psel"], writes=["Da2i"], dma=True)
        def a2a_coll(e):
            from concourse.aot_env import allow_ici_all_to_all
            with allow_ici_all_to_all():
                return e.collective_compute(
                    "AllToAll", ALU.bypass, replica_groups=RGROUPS8,
                    ins=[a2a_in.opt()], outs=[a2a_out.opt()])
        P.step("gpsimd", a2a_coll, reads=["Da2i"], writes=["Da2o"], collective=True)
        def nrm_out(e):
            r = dyn_off(e, "sy_a2a2", EC * TQ, 0)
            ap = bass.AP(a2a_out.tensor, r, [[1, 4 * EC * TQ]])
            return e.dma_start(out=ytq_d.rearrange("r t -> (r t)"), in_=ap)
        P.step("sync", nrm_out, reads=["Da2o", "psel"], writes=["Dytq"], dma=True)
        P.barrier()
        A.reset(COMMON)
        yt_t = [A.take_bf16(TQ) for _ in range(2)]
        wo_t = [A.take_bf16(32 * 128) for _ in range(2)]
        xq_t = A.take_f32(512)
        x1t = A.take_bf16(TQ)
        for db in range(16):
            dma("gpsimd", wo_t[db % 2][:], wout_l[db].rearrange("p k c -> p (k c)"),
                w=[f"wo{db % 2}"])
            for kc in range(32):
                dma("sync", yt_t[kc % 2][:], ytq_d[kc * 128:(kc + 1) * 128, :],
                    r=["Dytq"], w=[f"yt{kc % 2}"])
                for tt in range(4):
                    mm(lambda e, db=db, kc=kc, tt=tt: e.matmul(
                        ps(tt), wo_t[db % 2][:, kc * 128:kc * 128 + 128],
                        yt_t[kc % 2][:, tt * 512:tt * 512 + 512],
                        start=(kc == 0), stop=(kc == 31)),
                       r=[f"wo{db % 2}", f"yt{kc % 2}"], w=[f"p{tt}"])
            for tt in range(4):
                dma("sync", xq_t[:, 0:512], xqt[db * 128:(db + 1) * 128, tt * 512:tt * 512 + 512],
                    w=["xq"])
                vec(lambda e, tt=tt: e.tensor_tensor(
                    x1t[:, tt * 512:tt * 512 + 512], ps(tt), xq_t[:, 0:512], ALU.add),
                    r=[f"p{tt}", "xq"], w=["x1t"])
            dma("sync", x1_d[db * 128:(db + 1) * 128, :], x1t[:], r=["x1t"], w=["Dx1"])
            if want("x1"):
                store(tap_ap["x1"][db * 128:(db + 1) * 128, :], x1t[:], r=["x1t"])

    def ffn(src_d, dst_d, final_out, src_key="Dx1", dst_key="Dx2"):
        P.barrier()
        A.reset(COMMON)
        h_t = A.take_bf16(8 * 4 * TCH)
        acc32 = A.take_f32(16 * TCH)
        xn2 = A.take_bf16(16 * TCH)
        wgt = A.take_bf16(16 * 128)
        wut = A.take_bf16(16 * 128)
        wdt = [A.take_bf16(16 * 128)]
        thr_t = A.take_bf16(TCH)
        m1_t = A.take_bf16(2 * TCH)
        res_t = thr_t[:, 0:512]
        o32 = m1_t.bitcast(F32)[:, 0:512]

        def hview(ib8, m):
            return h_t[:, (ib8 * 4 + m) * TCH:(ib8 * 4 + m + 1) * TCH]

        for half in range(2):
            t0 = half * TCH
            ln_norm(src_d, True, t0, TCH, xn2, None, extra_rms=False,
                    src_key=src_key, xb_key="Fxn2")
            for ih in range(2):
                for ib8 in range(8):
                    ib = ih * 8 + ib8
                    for m in range(4):
                        dma("gpsimd", wgt[:], wg_l[ib, m].rearrange("p k c -> p (k c)"),
                            w=["Fwg"])
                        dma("gpsimd", wut[:], wu_l[ib, m].rearrange("p k c -> p (k c)"),
                            w=["Fwu"])
                        for tt in range(2):
                            for kc in range(16):
                                mm(lambda e, tt=tt, kc=kc: e.matmul(
                                    ps(0 + tt), wgt[:, kc * 128:kc * 128 + 128],
                                    xn2[:, kc * TCH + tt * 512:kc * TCH + tt * 512 + 512],
                                    start=(kc == 0), stop=(kc == 15)),
                                   r=["Fwg", "Fxn2"], w=[f"p{tt}"])
                            for kc in range(16):
                                mm(lambda e, tt=tt, kc=kc: e.matmul(
                                    ps(2 + tt), wut[:, kc * 128:kc * 128 + 128],
                                    xn2[:, kc * TCH + tt * 512:kc * TCH + tt * 512 + 512],
                                    start=(kc == 0), stop=(kc == 15)),
                                   r=["Fwu", "Fxn2"], w=[f"p{2 + tt}"])
                        hv = hview(ib8, m)
                        act(lambda e, hv=hv: e.activation(hv[:, 0:512], ps(0), AF.Silu),
                            r=["p0"], w=["Fh"])
                        act(lambda e, hv=hv: e.activation(hv[:, 512:1024], ps(1), AF.Silu),
                            r=["p1"], w=["Fh"])
                        vec(lambda e, hv=hv: e.tensor_tensor(hv[:, 0:512], hv[:, 0:512],
                                                             ps(2), ALU.mult),
                            r=["Fh", "p2"], w=["Fh"])
                        vec(lambda e, hv=hv: e.tensor_tensor(hv[:, 512:1024], hv[:, 512:1024],
                                                             ps(3), ALU.mult),
                            r=["Fh", "p3"], w=["Fh"])
                    h0, h1 = hview(ib8, 0), hview(ib8, 1)
                    h2v, h3 = hview(ib8, 2), hview(ib8, 3)
                    ab = [m1_t[:, 0:TCH], m1_t[:, TCH:2 * TCH]]
                    aa = [sq_c[:, 0:TCH], evtmp[:, 0:TCH], evtmp[:, TCH:2 * TCH], thr_t[:, 0:TCH]]
                    SPR = ["Fh", "sq", "ev", "Fthr", "Fm1"]
                    for idx4, hv in enumerate((h0, h1, h2v, h3)):
                        act(lambda e, hv=hv, idx4=idx4: e.activation(aa[idx4], hv, AF.Abs),
                            r=SPR, w=SPR[1:])
                    vec(lambda e: e.tensor_tensor(ab[0], aa[0], aa[1], ALU.max), r=SPR, w=["Fm1"])
                    vec(lambda e: e.tensor_tensor(aa[0], aa[0], aa[1], ALU.min), r=SPR, w=["sq"])
                    vec(lambda e: e.tensor_tensor(ab[1], aa[2], aa[3], ALU.max), r=SPR, w=["Fm1"])
                    vec(lambda e: e.tensor_tensor(aa[2], aa[2], aa[3], ALU.min), r=SPR, w=["ev"])
                    vec(lambda e: e.tensor_tensor(ab[0], ab[0], ab[1], ALU.min), r=SPR, w=["Fm1"])
                    vec(lambda e: e.tensor_tensor(aa[0], aa[0], aa[2], ALU.max), r=SPR, w=["sq"])
                    vec(lambda e: e.tensor_tensor(ab[0], ab[0], aa[0], ALU.max), r=SPR, w=["Fm1"])
                    for hv in (h0, h1, h2v, h3):
                        act(lambda e, hv=hv: e.activation(aa[1], hv, AF.Abs), r=SPR, w=["ev"])
                        vec(lambda e, hv=hv: e.tensor_tensor(aa[2], aa[1], ab[0], ALU.is_ge),
                            r=SPR, w=["ev"])
                        vec(lambda e, hv=hv: e.tensor_tensor(hv, hv, aa[2], ALU.mult),
                            r=SPR, w=["Fh"])
                for bg in range(4):
                    for ibm in range(32):
                        ib8, m = ibm // 4, ibm % 4
                        ib = ih * 8 + ib8
                        dma("gpsimd", wdt[0][:, 0:512],
                            wdn_l[ib, m, :, bg * 4:bg * 4 + 4, :].rearrange("p d c -> p (d c)"),
                            w=["Fwd"])
                        hv = hview(ib8, m)
                        for dj in range(4):
                            for tt in range(2):
                                mm(lambda e, ibm=ibm, dj=dj, tt=tt, hv=hv: e.matmul(
                                    ps(dj * 2 + tt), wdt[0][:, dj * 128:dj * 128 + 128],
                                    hv[:, tt * 512:tt * 512 + 512],
                                    start=(ibm == 0), stop=(ibm == 31)),
                                   r=["Fwd", "Fh"], w=[f"p{dj * 2 + tt}"])
                    for dj in range(4):
                        db = bg * 4 + dj
                        for tt in range(2):
                            a32 = acc32[:, (db * 2 + tt) * 512:(db * 2 + tt) * 512 + 512]
                            if ih == 0:
                                vec(lambda e, dj=dj, tt=tt, a32=a32: e.tensor_copy(
                                    a32, ps(dj * 2 + tt)),
                                    r=[f"p{dj * 2 + tt}"], w=["Facc"])
                            else:
                                dma("sync", res_t[:, 0:512],
                                    src_d[db * 128:(db + 1) * 128,
                                          t0 + tt * 512:t0 + tt * 512 + 512],
                                    r=[src_key], w=["Fthr"])
                                vec(lambda e, dj=dj, tt=tt, a32=a32: e.tensor_tensor(
                                    a32, a32, ps(dj * 2 + tt), ALU.add),
                                    r=["Facc", f"p{dj * 2 + tt}"], w=["Facc"])
                                if final_out:
                                    vec(lambda e, a32=a32: e.tensor_tensor(
                                        o32[:, 0:512], a32, res_t[:, 0:512], ALU.add),
                                        r=["Facc", "Fthr"], w=["Fm1"])
                                    dma("sync", out_t[db * 128:(db + 1) * 128,
                                                      t0 + tt * 512:t0 + tt * 512 + 512],
                                        o32[:, 0:512], r=["Fm1"], w=["OUT"])
                                else:
                                    vec(lambda e, a32=a32: e.tensor_tensor(
                                        sq_c[:, 0:512], a32, res_t[:, 0:512], ALU.add),
                                        r=["Facc", "Fthr"], w=["sq"])
                                    dma("sync", dst_d[db * 128:(db + 1) * 128,
                                                      t0 + tt * 512:t0 + tt * 512 + 512],
                                        sq_c[:, 0:512], r=["sq"], w=[dst_key])

    ATTN_PHASE = globals().get("ATTN_PHASE_LIMIT", 3)

    def attn():
        P.barrier()
        A.reset(COMMON)
        xn3 = A.take_bf16(16 * TCH)
        wqt = [A.take_bf16(16 * 128) for _ in range(2)]
        qkv_ev = A.take_bf16(TCH)
        for half in range(2):
            t0 = half * TCH
            ln_norm(x2_d, True, t0, TCH, xn3, None, extra_rms=False,
                    src_key="Dx2", xb_key="Txn3")
            if want("xn3"):
                for kc in range(16):
                    store(tap_ap["xn3"][kc * 128:(kc + 1) * 128, t0:t0 + TCH],
                          xn3[:, kc * TCH:(kc + 1) * TCH], r=["Txn3"])
            for h in range(NH):
                for wl, dst, dkey in ((wq_l, q_d, "Dqd"), (wk_l, kag_in, "Dkgi")):
                    dma("gpsimd", wqt[0][:], wl[h].rearrange("p k c -> p (k c)"), w=["Twq"])
                    for tt in range(2):
                        for kc in range(16):
                            mm(lambda e, tt=tt, kc=kc: e.matmul(
                                ps(0 + tt), wqt[0][:, kc * 128:kc * 128 + 128],
                                xn3[:, kc * TCH + tt * 512:kc * TCH + tt * 512 + 512],
                                start=(kc == 0), stop=(kc == 15)),
                               r=["Twq", "Txn3"], w=[f"p{tt}"])
                    vec(lambda e: e.tensor_copy(qkv_ev[:, 0:512], ps(0)), r=["p0"], w=["Tqe"])
                    vec(lambda e: e.tensor_copy(qkv_ev[:, 512:1024], ps(1)), r=["p1"], w=["Tqe"])
                    dma("sync", dst[h * HN:(h + 1) * HN, t0:t0 + TCH], qkv_ev[:, 0:TCH],
                        r=["Tqe"], w=[dkey])
                dma("gpsimd", wqt[1][:], wv_r[h].rearrange("p k c -> p (k c)"), w=["Twv"])
                for tb in range(TCH // 128):
                    for kc in range(16):
                        mm(lambda e, tb=tb, kc=kc: e.matmul(
                            ps(2, w=128), xn3[:, kc * TCH + tb * 128:kc * TCH + tb * 128 + 128],
                            wqt[1][:, kc * 128:kc * 128 + 128],
                            start=(kc == 0), stop=(kc == 15)),
                           r=["Twv", "Txn3"], w=["p2"])
                    vec(lambda e, tb=tb: e.tensor_copy(qkv_ev[:, 0:128], ps(2, w=128)),
                        r=["p2"], w=["Tqe"])
                    dma("sync", vag_in[h * TQ + t0 + tb * 128: h * TQ + t0 + (tb + 1) * 128, :],
                        qkv_ev[:, 0:128], r=["Tqe"], w=["Dvgi"])
        P.step("gpsimd", lambda e: e.collective_compute(
            "AllGather", ALU.bypass, replica_groups=RGROUPS8,
            ins=[kag_in.opt()], outs=[kag_out.opt()]),
            reads=["Dkgi"], writes=["Dkgo"], collective=True)
        P.step("gpsimd", lambda e: e.collective_compute(
            "AllGather", ALU.bypass, replica_groups=RGROUPS8,
            ins=[vag_in.opt()], outs=[vag_out.opt()]),
            reads=["Dvgi"], writes=["Dvgo"], collective=True)
        def nrm_k(e):
            r = dyn_off(e, "sy_kv", NH * HN * TQ, 0)
            ap = bass.AP(kag_out.tensor, r, [[1, 4 * NH * HN * TQ]])
            return e.dma_start(out=kq_d.rearrange("r t -> (r t)"), in_=ap)
        P.step("sync", nrm_k, reads=["Dkgo", "psel"], writes=["Dkq"], dma=True)
        def nrm_v(e):
            r = dyn_off(e, "sy_kv2", NH * TQ * HN, 0)
            ap = bass.AP(vag_out.tensor, r, [[1, 4 * NH * TQ * HN]])
            return e.dma_start(out=vq_d.rearrange("r t -> (r t)"), in_=ap)
        P.step("sync", nrm_v, reads=["Dvgo", "psel"], writes=["Dvq"], dma=True)
        if ATTN_PHASE < 2:
            return
        P.barrier()
        A.reset(COMMON)
        kh_t = A.take_bf16(L)
        vh_t = [A.take_bf16(HN) for _ in range(2)]
        qh_t = A.take_bf16(TQ)
        pb_t = A.take_bf16(TCH)
        rb_t = A.take_bf16(512)
        o_ev = A.take_bf16(TQ)

        def kread(e, h, dst):
            ap = bass.AP(kq_d.tensor, h * HN * TQ, [[TQ, 128], [NH * HN * TQ, 4], [1, TQ]])
            return e.dma_start(out=dst.rearrange("p (r t) -> p r t", r=4), in_=ap)

        def vread(e, h, kt, dst):
            rank, off = kt // 16, (kt % 16) * 128
            ap = bass.AP(vq_d.tensor, (rank * NH * TQ + h * TQ + off) * HN, [[HN, 128], [1, HN]])
            return e.dma_start(out=dst, in_=ap)

        for h in range(globals().get("ATTN_NH", NH)):
            P.step("sync", lambda e, h=h: kread(e, h, kh_t[:]),
                   reads=["Dkq"], writes=["Tkh"], dma=True)
            dma("sync", qh_t[:], q_d[h * HN:(h + 1) * HN, :], r=["Dqd"], w=["Tqh"])
            for qt in range(globals().get("ATTN_NQ", 4)):
                NKT = globals().get("ATTN_NKT", 64)
                for kt in range(NKT):
                    sb_ = 0 if (kt % 2 == 0) else 2
                    mm(lambda e, kt=kt, qt=qt, sb_=sb_: e.matmul(
                        ps(sb_, w=512), kh_t[:, kt * 128:kt * 128 + 128],
                        qh_t[:, qt * 512:qt * 512 + 512], start=True, stop=True),
                       r=["Tkh", "Tqh"], w=[f"p{sb_}"])
                    act(lambda e, sb_=sb_: e.activation(
                        pb_t[:, 0:512], ps(sb_), AF.Exp, scale=1.0 / np.sqrt(HN),
                        bias=cst32[:, 0:1]),
                        r=[f"p{sb_}", "cst32"], w=["Tpb"])
                    P.step("sync", lambda e, h=h, kt=kt: vread(e, h, kt, vh_t[kt % 2][:, 0:HN]),
                           reads=["Dvq"], writes=[f"Tvh{kt % 2}"], dma=True)
                    mm(lambda e, kt=kt, NKT=NKT: e.matmul(
                        ps(4), vh_t[kt % 2][:, 0:HN], pb_t[:, 0:512],
                        start=(kt == 0), stop=(kt == NKT - 1)),
                       r=[f"Tvh{kt % 2}", "Tpb"], w=["p4"])
                    mm(lambda e, kt=kt, NKT=NKT: e.matmul(
                        ps(5, nr=1), onesb[:, 0:1], pb_t[:, 0:512],
                        start=(kt == 0), stop=(kt == NKT - 1)),
                       r=["Tpb", "consts"], w=["p5"])
                vec(lambda e, qt=qt: e.tensor_copy(rows[0:1, 0:512], ps(5, nr=1)),
                    r=["p5"], w=["rows"])
                vec(lambda e: e.reciprocal(rows[0:1, 0:512], rows[0:1, 0:512]),
                    r=["rows"], w=["rows"])
                vec(lambda e: e.tensor_copy(rows2[0:1, 0:512], rows[0:1, 0:512]),
                    r=["rows"], w=["rows2"])
                mm(lambda e: e.matmul(ps(6), onesb[0:1, 0:128], rows2[0:1, 0:512],
                                      start=True, stop=True),
                   r=["rows2", "consts"], w=["p6"])
                vec(lambda e: e.tensor_copy(rb_t[:, 0:512], ps(6)), r=["p6"], w=["Trb"])
                vec(lambda e, qt=qt: e.tensor_tensor(
                    o_ev[:, qt * 512:qt * 512 + 512], ps(4), rb_t[:, 0:512], ALU.mult),
                    r=["p4", "Trb"], w=["Toe"])
            slot = h // 2 + (h % 2) * 8
            dma("sync", q_d[slot * HN:(slot + 1) * HN, :], o_ev[:], r=["Toe"], w=["Dqd"])
        if ATTN_PHASE < 3:
            return
        def ctx_stage(e):
            r = dyn_off(e, "sy_ctx_s", 1, 0, pidx=1)
            ap = bass.AP(q_d.tensor, r, [[1, 8 * HN * TQ]])
            return e.dma_start(out=agc_in.rearrange("r t -> (r t)"), in_=ap)
        P.step("sync", ctx_stage, reads=["Dqd", "psel"], writes=["Dagi"], dma=True)
        P.step("gpsimd", lambda e: e.collective_compute(
            "AllGather", ALU.bypass, replica_groups=RGROUPS8,
            ins=[agc_in.opt()], outs=[agc_out.opt()]),
            reads=["Dagi"], writes=["Dago"], collective=True)
        def ctx_own(e):
            r = dyn_off(e, "sy_ctx_o", 1, 0, pidx=2)
            ap = bass.AP(q_d.tensor, r, [[1, 8 * HN * TQ]])
            return e.dma_start(out=ctxp_d[0:8 * HN, :].rearrange("r t -> (r t)"), in_=ap)
        P.step("sync", ctx_own, reads=["Dqd", "psel"], writes=["Dctxp"], dma=True)
        def ctx_part(e):
            r = dyn_off(e, "sy_ctx_p", 1, 0, pidx=3)
            ap = bass.AP(agc_out.tensor, r, [[1, 8 * HN * TQ]])
            return e.dma_start(out=ctxp_d[8 * HN:16 * HN, :].rearrange("r t -> (r t)"), in_=ap)
        P.step("sync", ctx_part, reads=["Dago", "psel"], writes=["Dctxp"], dma=True)
        P.barrier()
        A.reset(COMMON)
        wpt = [A.take_bf16(16 * 128) for _ in range(2)]
        ctx_s = A.take_bf16(TQ)
        res_t = A.take_bf16(512)
        x3_t = A.take_bf16(512)
        for db in range(16):
            dma("gpsimd", wpt[db % 2][:], wproj_l[db].rearrange("p k c -> p (k c)"),
                w=[f"Twp{db % 2}"])
            for j in range(NH):
                dma("sync", ctx_s[:], ctxp_d[j * HN:(j + 1) * HN, :], r=["Dctxp"], w=["Tcs"])
                for tt in range(4):
                    mm(lambda e, db=db, j=j, tt=tt: e.matmul(
                        ps(tt), wpt[db % 2][:, j * 128:j * 128 + 128],
                        ctx_s[:, tt * 512:tt * 512 + 512],
                        start=(j == 0), stop=(j == NH - 1)),
                       r=[f"Twp{db % 2}", "Tcs"], w=[f"p{tt}"])
            for tt in range(4):
                dma("sync", res_t[:, 0:512], x2_d[db * 128:(db + 1) * 128, tt * 512:tt * 512 + 512],
                    r=["Dx2"], w=["Tres"])
                vec(lambda e, db=db, tt=tt: e.scalar_tensor_tensor(
                    x3_t[:, 0:512], ps(tt), bprj_sb[:, db:db + 1], res_t[:, 0:512],
                    ALU.add, ALU.add),
                    r=[f"p{tt}", "bprj", "Tres"], w=["Tx3"])
                dma("sync", x3_d[db * 128:(db + 1) * 128, tt * 512:tt * 512 + 512], x3_t[:, 0:512],
                    r=["Tx3"], w=["Dx3"])
                if want("x3"):
                    store(tap_ap["x3"][db * 128:(db + 1) * 128, tt * 512:tt * 512 + 512],
                          x3_t[:, 0:512], r=["Tx3"])

    order = ["mamba", "mamba_out", "ffn1", "attn", "ffn2"]
    stop_at = {"mamba": 1, "x1": 2, "ffn1": 3, "attn": 4, "all": 5}[upto]
    mamba()
    if stop_at >= 2:
        mamba_out()
    if stop_at >= 3:
        ffn(x1_d, x2_d, final_out=False, src_key="Dx1", dst_key="Dx2")
        if want("x2"):
            for db in range(16):
                dma("sync", evtmp[:, 0:TQ], x2_d[db * 128:(db + 1) * 128, :], r=["Dx2"], w=["ev"])
                store(tap_ap["x2"][db * 128:(db + 1) * 128, :], evtmp[:, 0:TQ], r=["ev"])
    if stop_at >= 4:
        attn()
    if stop_at >= 5:
        ffn(x3_d, None, final_out=True, src_key="Dx3", dst_key="OUT")
    else:
        ev32z = evtmp.bitcast(F32)
        vec(lambda e: e.memset(ev32z[:, 0:TCH], 0.0), r=["ev"], w=["ev"])
        for db in range(16):
            for hf in range(2):
                store(out_t[db * 128:(db + 1) * 128, hf * TCH:(hf + 1) * TCH], ev32z[:, 0:TCH],
                      r=["ev"])

    P.barrier()
    P.step("sync", lambda e: None)
    P.emit()
    return nc, tap_ap


def kernel(**inputs):
    in_maps = stage_inputs(inputs)
    nc, _ = build(serial=False, upto="all", taps=())
    res = run_bass_kernel_spmd(nc, in_maps, core_ids=list(range(NCORE)))
    out = np.empty((B_, L, D), dtype=np.float32)
    for c in range(NCORE):
        b, q = c // 4, c % 4
        out[b, q * TQ:(q + 1) * TQ, :] = np.asarray(res.results[c]["out_t"]).T
    return out
